# version 07
# baseline (speedup 1.0000x reference)
import numpy as np
import ml_dtypes
import concourse.bass as bass
import concourse.mybir as mybir
from concourse.bass_utils import run_bass_kernel_spmd

F32 = mybir.dt.float32
BF16 = mybir.dt.bfloat16
AF = mybir.ActivationFunctionType
ALU = mybir.AluOpType
AX = mybir.AxisListType

NEG = -1.0e30
SCALE_Q = float(192.0 ** -0.5)
LN_EPS = 1.0e-5


def full_cfg():
    return dict(D=4096, HID=11008, NH=32, NS=8, NCORES=8)


def _derived(cfg):
    D, HID, NH, NS = cfg["D"], cfg["HID"], cfg["NH"], cfg["NS"]
    T = NS * 128
    W = min(512, T)
    d = dict(
        D=D, HID=HID, NH=NH, NS=NS, T=T, W=W,
        TH=T // W,
        DC=D // 128,
        HCN=HID // 128,
        DG=D // 512,
        HW=min(D, 2048),
    )
    d["NHALF"] = D // d["HW"]
    return d


class _Sem:
    __slots__ = ("name", "n", "h")

    def __init__(self, name):
        self.name = name
        self.n = 0
        self.h = None


class _B:

    def __init__(self, nc, c):
        self.nc = nc
        self.c = c
        self.prog = {"t": [], "v": [], "s": [], "y": [], "g": []}
        self.PRE = _Sem("PRE")
        self.MSK = _Sem("MSK")
        self.sXT = _Sem("sXT")
        self.GST = [_Sem("GST0"), _Sem("GST1")]
        self.GSs = [_Sem("GSs0"), _Sem("GSs1"), _Sem("GSs2")]
        self.XCs = [_Sem("XCs0"), _Sem("XCs1")]
        self.XOs = [_Sem("XOs0"), _Sem("XOs1")]
        self.SILs = [_Sem("SILs0"), _Sem("SILs1")]
        self.VSTs = [_Sem("VSTs0"), _Sem("VSTs1")]
        self.KV = [_Sem("KV0"), _Sem("KV1")]
        self.AFSt = [_Sem("AFSt0"), _Sem("AFSt1")]
        self.WSm = [_Sem("WSm0"), _Sem("WSm1")]
        self.WB = [_Sem("WB0"), _Sem("WB1")]
        self.WSy = [_Sem("WSy0"), _Sem("WSy1")]
        self.W2g = [_Sem(f"W2g{i}") for i in range(3)]
        self.B13m = [_Sem("B13m0"), _Sem("B13m1")]
        self.W2C = _Sem("W2C")
        self.W2m = [_Sem(f"W2m{i}") for i in range(3)]
        self.T = _Sem("T")
        self.V = _Sem("V")
        self.S = _Sem("S")
        self.sems = ([self.PRE, self.MSK, self.sXT,
                      self.T, self.V, self.S]
                     + self.GST + self.GSs + self.XCs + self.XOs
                     + self.SILs + self.VSTs + self.KV + self.AFSt + self.WSm + self.W2m
                     + self.WB + self.WSy + self.B13m + self.W2g + [self.W2C])
        self._declare()
        self.bank_free = [(self.V, 0) for _ in range(8)]

    def w(self, e, sem, v):
        if v > 0:
            self.prog[e].append(lambda E, s=sem, vv=v: E.wait_ge(s.h, vv))

    def o(self, e, meth, *a, inc=None, k=1, **kw):
        if inc is not None:
            inc.n += k
        v = inc.n if inc is not None else 0

        def f(E, m=meth, aa=a, kk=kw, s=inc, n=k):
            r = getattr(E, m)(*aa, **kk)
            if s is not None:
                r.then_inc(s.h, n)

        self.prog[e].append(f)
        return v

    def dma(self, e, sem, dst, src, **kw):
        return self.o(e, "dma_start", inc=sem, k=16, out=dst, in_=src, **kw)

    def _declare(self):
        nc, c = self.nc, self.c
        D, HID, NH, NS, T = c["D"], c["HID"], c["NH"], c["NS"], c["T"]
        DC, HCN = c["DC"], c["HCN"]
        dt = nc.dram_tensor
        self.x0 = dt("x0", [T, D], F32, kind="ExternalInput")
        self.x0p = dt("x0p", [T, D], F32, kind="ExternalInput")
        self.w1t = dt("w1t", [HCN * 128, DC * 128], F32, kind="ExternalInput")
        self.w3t = dt("w3t", [HCN * 128, DC * 128], F32, kind="ExternalInput")
        self.w2t = dt("w2t", [HID, D], F32, kind="ExternalInput")
        self.wq = dt("wq", [NH * 128, DC * 128], F32, kind="ExternalInput")
        self.wkv = dt("wkv", [2 * NH * 128, DC * 128], F32, kind="ExternalInput")
        self.wo = dt("wo", [NH * 128, D], F32, kind="ExternalInput")
        self.a13 = dt("a13", [D, 32], F32, kind="ExternalInput")
        self.b13 = dt("b13", [16, 2 * HID], F32, kind="ExternalInput")
        self.a2 = dt("a2", [HID, 16], F32, kind="ExternalInput")
        self.b2 = dt("b2", [16, D], F32, kind="ExternalInput")
        self.msk = dt("msk", [NS * 128, 256], BF16, kind="ExternalInput")
        self.idn = dt("idn", [128, 128], F32, kind="ExternalInput")
        self.out = dt("out", [T, D], F32, kind="ExternalOutput")
        self.xr1 = dt("xr1", [T, D], F32)
        self.xr1p = dt("xr1p", [T, D], F32)
        self.xr2 = dt("xr2", [T, D], F32)
        self.gbuf = dt("gbuf", [HCN * 128, T], BF16)
        self.qloc = dt("qloc", [NH * 128, T], BF16)
        self.kloc = dt("kloc", [NH * 128, 2 * T], BF16)
        self.vloc = dt("vloc", [NH * 2 * T, 128], BF16)
        self.afm = dt("afm", [NH * 128, T], BF16)
        self.w1bf = dt("w1bf", [HCN * 128, DC * 128], BF16)
        self.w3bf = dt("w3bf", [HCN * 128, DC * 128], BF16)
        self.w2bf = dt("w2bf", [HID, D], BF16)

        st = nc.alloc_sbuf_tensor
        ACOL = max(DC, NH) * T
        self.arena = st("arena", [128, ACOL], BF16)
        self.WS1 = st("WS1", [128, 2 * DC * 128], BF16)
        self.WS3 = st("WS3", [128, 2 * DC * 128], BF16)
        self.W2S = st("W2S", [128, 3 * 512], BF16)
        self.GB = st("GB", [128, 2 * T], BF16)
        self.GS = st("GS", [128, 3 * T], BF16)
        self.SIL = st("SIL", [128, 2 * T], BF16)
        self.H3B = st("H3B", [128, 2 * T], BF16)
        self.XT = st("XT", [128, max(c["NHALF"] * c["HW"], NS * 512)], F32)
        self.XLT = st("XLT", [128, c["NHALF"] * c["HW"]], BF16)
        self.XO = st("XO", [128, 2 * 512], F32)
        self.A13S = st("A13S", [128, DC * 32], BF16)
        self.A2S = st("A2S", [128, HCN * 16], BF16)
        self.B13S = st("B13S", [16, 2 * 256], BF16)
        self.B2S = st("B2S", [16, D], BF16)
        self.S1B = st("S1B", [16, T], BF16)
        self.S3B = st("S3B", [16, T], BF16)
        self.S2B = st("S2B", [16, T], BF16)
        self.QH = st("QH", [128, 2 * T], BF16)
        self.KH = st("KH", [128, 2 * 2 * T], BF16)
        self.VH = st("VH", [128, 2 * 2 * T], BF16)
        self.PT = st("PT", [128, 2 * 2 * T], BF16)
        self.AFS = st("AFS", [128, 2 * 512], BF16)
        self.PF = st("PF", [128, 2 * 2 * T], BF16)
        self.MSKS = st("MSKS", [128, NS * 256], BF16)
        self.STT = st("STT", [128, 32], F32)
        self.CON = st("CON", [128, 8], F32)
        self.SFX = st("SFX", [128, 48], F32)
        self.IDN = st("IDN", [128, 128], BF16)

        self.PS = [nc.alloc_psum_tensor(f"PS{i}", [128, 512], F32) for i in range(8)]
        self.PSb = [p.ap().bitcast(BF16) for p in self.PS]

        z = lambda: [(self.V, 0), (self.V, 0)]
        self.fr = {k: z() for k in (
            "WS1", "WS3", "GB", "SIL", "H3B", "XO",
            "B13S", "QH", "KH", "VH", "PT", "PF")}
        self.fr["GS"] = [(self.V, 0)] * 3
        self.fr["W2S"] = [(self.V, 0)] * 3
        self.fr["AFS"] = [(self.V, 0), (self.V, 0)]
        self.fr["XT"] = [(self.V, 0)] * c["NHALF"]
        self.fr["XC8"] = [(self.V, 0)] * NS
        self.fr["XLT"] = [(self.V, 0)] * c["NHALF"]

    def prelude(self):
        c = self.c
        DC, HCN, D, NS = c["DC"], c["HCN"], c["D"], c["NS"]
        for i, val in enumerate((LN_EPS, 2.0, 0.5, SCALE_Q, -1.0)):
            ap = self.CON[:, i:i + 1]
            self.o("v", "memset", ap, val,
                   inc=(self.V if i == 4 else None))
            self.nc.const_aps.register(F32, val, ap)
        self.const_V = self.V.n
        self.dma("g", self.PRE, self.IDN[:, :], self.idn[:, :])
        self.dma("g", self.PRE, self.A13S[:, :],
                 bass.AP(self.a13, 0, [[32, 128], [128 * 32, DC], [1, 32]]))
        self.dma("g", self.PRE, self.A2S[:, :],
                 bass.AP(self.a2, 0, [[16, 128], [128 * 16, HCN], [1, 16]]))
        self.dma("g", self.PRE, self.B2S[:, :], self.b2[:, :])
        self.dma("y", self.MSK, self.MSKS[:, :],
                 bass.AP(self.msk, 0, [[256, 128], [128 * 256, NS], [1, 256]]))
        for e in ("t", "s", "v"):
            self.w(e, self.V, self.const_V)
            self.w(e, self.PRE, self.PRE.n)
            self.w(e, self.MSK, self.MSK.n)

    def phase_a(self, xsrc, src_ready):
        c = self.c
        NS, D, HW, NHALF, T = c["NS"], c["D"], c["HW"], c["NHALF"], c["T"]
        nbank = 0
        norm_cnt = {}
        trans_cnt = {}
        for tau in range(NS):
            SB = (tau % 2) * 16
            ld = {}
            for h in range(NHALF):
                self._wait_fr("y", self.fr["XT"][h])
                if src_ready is not None and src_ready[tau] is not None:
                    self.w("y", src_ready[tau][0], src_ready[tau][1])
                ld[h] = self.dma(
                    "y", self.sXT, self.XT[:, h * HW:(h + 1) * HW],
                    bass.AP(xsrc, tau * 128 * D + h * HW, [[D, 128], [1, HW]]))
            ld_last = ld[NHALF - 1]
            for h in range(NHALF):
                self.w("v", self.sXT, ld_last)
                self.o("v", "reduce_sum", self.STT[:, SB + h:SB + h + 1],
                       self.XT[:, h * HW:(h + 1) * HW], AX.X, negate=True, inc=self.V)
                self.w("s", self.sXT, ld_last)
                self._wait_fr("s", self.fr["XLT"][h])
                self.o("s", "activation", self.XLT[:, h * HW:(h + 1) * HW],
                       self.XT[:, h * HW:(h + 1) * HW], AF.Square,
                       accum_out=self.STT[:, SB + 2 + h:SB + 3 + h], inc=self.S)
            sq_done = self.S.n
            self.w("v", self.S, sq_done)
            self.w("v", self.V, self.V.n)
            if NHALF == 2:
                self.o("v", "tensor_add", self.STT[:, SB + 4:SB + 5],
                       self.STT[:, SB:SB + 1], self.STT[:, SB + 1:SB + 2], inc=self.V)
                self.o("v", "tensor_add", self.STT[:, SB + 5:SB + 6],
                       self.STT[:, SB + 2:SB + 3], self.STT[:, SB + 3:SB + 4], inc=self.V)
                nsc, sqc = SB + 4, SB + 5
            else:
                nsc, sqc = SB + 0, SB + 2
            self.w("v", self.V, self.V.n)
            self.o("v", "tensor_scalar_mul", self.STT[:, SB + 6:SB + 7],
                   self.STT[:, nsc:nsc + 1], 1.0 / D, inc=self.V)
            self.o("v", "tensor_scalar_mul", self.STT[:, SB + 7:SB + 8],
                   self.STT[:, sqc:sqc + 1], 1.0 / D, inc=self.V)
            self.w("v", self.V, self.V.n)
            self.o("v", "tensor_mul", self.STT[:, SB + 8:SB + 9],
                   self.STT[:, SB + 6:SB + 7], self.STT[:, SB + 6:SB + 7], inc=self.V)
            self.w("v", self.V, self.V.n)
            self.o("v", "tensor_sub", self.STT[:, SB + 9:SB + 10],
                   self.STT[:, SB + 7:SB + 8], self.STT[:, SB + 8:SB + 9], inc=self.V)
            st1 = self.V.n
            self.w("s", self.V, st1)
            self.o("s", "activation", self.STT[:, SB + 10:SB + 11],
                   self.STT[:, SB + 9:SB + 10], AF.Sqrt, bias=LN_EPS, inc=self.S)
            self.w("v", self.S, self.S.n)
            self.o("v", "reciprocal", self.STT[:, SB + 11:SB + 12],
                   self.STT[:, SB + 10:SB + 11], inc=self.V)
            self.w("v", self.V, self.V.n)
            self.o("v", "tensor_mul", self.STT[:, SB + 12:SB + 13],
                   self.STT[:, SB + 6:SB + 7], self.STT[:, SB + 11:SB + 12], inc=self.V)
            stats_done = self.V.n
            self.w("s", self.V, stats_done)
            for h in range(NHALF):
                norm_cnt[(tau, h)] = self.o(
                    "s", "activation", self.XLT[:, h * HW:(h + 1) * HW],
                    self.XT[:, h * HW:(h + 1) * HW], AF.Identity,
                    bias=self.STT[:, SB + 12:SB + 13],
                    scale=self.STT[:, SB + 11:SB + 12], inc=self.S)
                self.fr["XT"][h] = (self.S, norm_cnt[(tau, h)])
            for h in range(NHALF):
                self.w("t", self.S, norm_cnt[(tau, h)])
                nch = HW // 128
                i = 0
                while i < nch:
                    nb = min(8, nch - i)
                    bk = nbank % 4
                    nbank += 1
                    s, v = self.bank_free[bk]
                    self.w("t", s, v)
                    for q in range(nb):
                        self.o("t", "transpose",
                               self.PSb[bk][:, q * 128:(q + 1) * 128],
                               self.XLT[:, h * HW + (i + q) * 128: h * HW + (i + q + 1) * 128],
                               self.IDN[:, :],
                               inc=(self.T if q == nb - 1 else None))
                    tcnt = self.T.n
                    dcg0 = h * (HW // 128) + i
                    self.w("v", self.T, tcnt)
                    ev = self.o("v", "tensor_copy",
                                bass.AP(self.arena, dcg0 * T + tau * 128,
                                        [[self.arena.shape[1], 128], [T, nb], [1, 128]]),
                                self.PSb[bk][:, 0:nb * 128], inc=self.V)
                    self.bank_free[bk] = (self.V, ev)
                    i += nb
                trans_cnt[(tau, h)] = self.T.n
                self.fr["XLT"][h] = (self.T, trans_cnt[(tau, h)])
        self.xln_ready = self.V.n
        last_norm = max(norm_cnt.values())
        self.fr["XC8"] = [(self.S, last_norm)] * NS

    def phase_b(self, convert=False):
        c = self.c
        DC, HCN, T, W, TH = c["DC"], c["HCN"], c["T"], c["W"], c["TH"]
        HID = c["HID"]
        CW = DC * 128
        s_ev = {}
        for (sbuf, colofs) in ((self.S1B, 0), (self.S3B, 16)):
            for w in range(TH):
                bk = 6 + w
                s, v = self.bank_free[bk]
                self.w("t", s, v)
                self.w("t", self.V, self.xln_ready)
                if colofs == 16:
                    self.w("t", self.S, s_ev[(0, w)])
                for dc in range(DC):
                    self.o("t", "matmul", self.PS[bk][0:16, 0:W],
                           self.A13S[:, dc * 32 + colofs: dc * 32 + colofs + 16],
                           self.arena[:, dc * T + w * W: dc * T + (w + 1) * W],
                           start=(dc == 0), stop=(dc == DC - 1),
                           inc=(self.T if dc == DC - 1 else None))
                self.w("s", self.T, self.T.n)
                s_ev[(0 if colofs == 0 else 1, w)] = self.o(
                    "s", "activation", sbuf[:, w * W:(w + 1) * W],
                    self.PS[bk][0:16, 0:W], AF.Copy, scale=2.0, inc=self.S)
                self.bank_free[bk] = (self.S, s_ev[(0 if colofs == 0 else 1, w)])
        s13_done = self.S.n

        h_mm = {}
        mul_cnt = {}
        gst = {}
        s2_mm = {}
        wld = {}
        wbst = {}

        def issue_loads(cc):
            p = cc % 2
            q = "g" if convert else "y"
            wsem = self.WSm[p] if convert else self.WSy[p]
            wv = 0
            for (srcF, srcB, buf, frk) in ((self.w1t, self.w1bf, self.WS1, "WS1"),
                                           (self.w3t, self.w3bf, self.WS3, "WS3")):
                self._wait_fr(q, self.fr[frk][p])
                if convert:
                    wv = self.dma("g", wsem, buf[:, p * CW:(p + 1) * CW],
                                  bass.AP(srcF, cc * 128 * CW, [[CW, 128], [1, CW]]))
                else:
                    wv = self.dma("y", wsem, buf[:, p * CW:(p + 1) * CW],
                                  bass.AP(srcB, cc * 128 * CW, [[CW, 128], [1, CW]]))
            self._wait_fr("g", self.fr["B13S"][p])
            bv = self.dma(
                "g", self.B13m[p], self.B13S[:, p * 256:(p + 1) * 256],
                bass.AP(self.b13, cc * 128, [[2 * HID, 16], [HID, 2], [1, 128]]))
            wld[cc] = (wsem, wv, self.B13m[p], bv)
            if convert:
                self.w("y", wld[cc][0], wld[cc][1])
                self.w("y", wld[cc][2], wld[cc][3])
                for (srcB, buf) in ((self.w1bf, self.WS1), (self.w3bf, self.WS3)):
                    wbst[cc] = self.dma(
                        "y", self.WB[p],
                        bass.AP(srcB, cc * 128 * CW, [[CW, 128], [1, CW]]),
                        buf[:, p * CW:(p + 1) * CW])

        if not convert and hasattr(self, "wb_total"):
            for sm, vv in self.wb_total:
                self.w("y", sm, vv)
        issue_loads(0)
        if HCN > 1:
            issue_loads(1)
        HB = lambda cc, i: (2 * TH * cc + i) % 6
        NPC = 16
        w2n = self.c["HID"] * self.c["D"]
        for cc in range(HCN):
            p = cc % 2
            if convert and cc >= 2 and (cc - 2) % 2 == 0 and (cc - 2) // 2 < NPC:
                i = (cc - 2) // 2
                per = w2n // NPC
                self.dma("g", self.W2C,
                         bass.AP(self.w2bf, i * per, [[per // 128, 128], [1, per // 128]]),
                         bass.AP(self.w2t, i * per, [[per // 128, 128], [1, per // 128]]))
            if cc >= 1:
                self._emit_s2(cc - 1, mul_cnt, s2_mm)
            self.w("t", wld[cc][0], wld[cc][1])
            self.w("t", wld[cc][2], wld[cc][3])
            if cc == 0:
                self.w("t", self.S, s13_done)
            banks = [HB(cc, i) for i in range(2 * TH)]
            for b in banks:
                self._wait_fr("t", self.bank_free[b])
            for half, (wsb, sreg) in enumerate(((self.WS1, self.S1B), (self.WS3, self.S3B))):
                for w in range(TH):
                    bk = banks[half * TH + w]
                    for dc in range(DC):
                        self.o("t", "matmul", self.PS[bk][:, 0:W],
                               wsb[:, p * DC * 128 + dc * 128: p * DC * 128 + (dc + 1) * 128],
                               self.arena[:, dc * T + w * W: dc * T + (w + 1) * W],
                               start=(dc == 0), stop=False)
                    self.o("t", "matmul", self.PS[bk][:, 0:W],
                           self.B13S[0:16, p * 256 + half * 128: p * 256 + half * 128 + 128],
                           sreg[0:16, w * W:(w + 1) * W],
                           start=False, stop=True, inc=self.T)
                    h_mm[(cc, half, w)] = self.T.n
            h_mm[cc] = self.T.n
            mk = (self.T, h_mm[cc])
            if convert:
                mk = ("both", mk, (self.WB[p], wbst[cc]))
            self.fr["WS1"][p] = mk
            self.fr["WS3"][p] = mk
            self.fr["B13S"][p] = (self.T, h_mm[cc])
            if cc + 2 < HCN:
                issue_loads(cc + 2)
            self._wait_fr("s", self.fr["SIL"][p])
            self._wait_fr("s", self.fr["H3B"][p])
            for w in range(TH):
                bk = banks[w]
                self.w("s", self.T, h_mm[(cc, 0, w)])
                e = self.o("s", "activation", self.SIL[:, p * T + w * W: p * T + (w + 1) * W],
                           self.PS[bk][:, 0:W], AF.Silu, inc=self.S)
                self.bank_free[bk] = (self.S, e)
            for w in range(TH):
                bk = banks[TH + w]
                self.w("s", self.T, h_mm[(cc, 1, w)])
                e = self.o("s", "activation", self.H3B[:, p * T + w * W: p * T + (w + 1) * W],
                           self.PS[bk][:, 0:W], AF.Copy, scale=0.5, inc=self.S)
                self.bank_free[bk] = (self.S, e)
            ev_done = self.S.n
            self._wait_fr("v", self.fr["GB"][p])
            self.w("v", self.S, ev_done)
            mul_cnt[cc] = self.o("v", "tensor_mul", self.GB[:, p * T:(p + 1) * T],
                                 self.SIL[:, p * T:(p + 1) * T],
                                 self.H3B[:, p * T:(p + 1) * T], inc=self.V)
            self.fr["SIL"][p] = (self.V, mul_cnt[cc])
            self.fr["H3B"][p] = (self.V, mul_cnt[cc])
            self.w("y", self.V, mul_cnt[cc])
            if hasattr(self, "gbuf_read_done"):
                for sm, vv in self.gbuf_read_done:
                    self.w("y", sm, vv)
                del self.gbuf_read_done
            gst[cc] = self.dma("y", self.GST[p],
                               bass.AP(self.gbuf, cc * 128 * T, [[T, 128], [1, T]]),
                               self.GB[:, p * T:(p + 1) * T])
            self.fr["GB"][p] = (self.GST[p], gst[cc])
        self._emit_s2(HCN - 1, mul_cnt, s2_mm)
        for w in range(TH):
            bk = 6 + w
            self.w("s", self.T, s2_mm[HCN - 1])
            e = self.o("s", "activation", self.S2B[:, w * W:(w + 1) * W],
                       self.PS[bk][0:16, 0:W], AF.Copy, scale=2.0, inc=self.S)
            self.bank_free[bk] = (self.S, e)
        self.s2_ready = self.S.n
        self.gst = gst
        if convert:
            done = max(0, min(NPC, (HCN - 2 + 1) // 2))
            per = w2n // NPC
            for i in range(done, NPC):
                self.dma("g", self.W2C,
                         bass.AP(self.w2bf, i * per, [[per // 128, 128], [1, per // 128]]),
                         bass.AP(self.w2t, i * per, [[per // 128, 128], [1, per // 128]]))
            self.w2c_total = self.W2C.n
            self.wb_total = [(self.WB[0], self.WB[0].n), (self.WB[1], self.WB[1].n)]

    def _emit_s2(self, cc, mul_cnt, s2_mm):
        c = self.c
        TH, W, T, HCN = c["TH"], c["W"], c["T"], c["HCN"]
        p = cc % 2
        self.w("t", self.V, mul_cnt[cc])
        for w in range(TH):
            self.o("t", "matmul", self.PS[6 + w][0:16, 0:W],
                   self.A2S[:, cc * 16:(cc + 1) * 16],
                   self.GB[:, p * T + w * W: p * T + (w + 1) * W],
                   start=(cc == 0), stop=(cc == HCN - 1),
                   inc=(self.T if w == TH - 1 else None))
        s2_mm[cc] = self.T.n
        self.fr["GB"][p] = ("both", (self.T, s2_mm[cc]), self.fr["GB"][p])

    def _wait_fr(self, e, marker):
        if isinstance(marker[0], str) and marker[0] == "both":
            self._wait_fr(e, marker[1])
            self._wait_fr(e, marker[2])
        else:
            self.w(e, marker[0], marker[1])

    def phase_c(self, xsrc, xdst, weight, nchunks, lhs_of_chunk, stream_g, lora,
                src_ready, final=False, wbf=None, gsrc=None, gready=None):
        c = self.c
        NS, T, D, DG = c["NS"], c["T"], c["D"], c["DG"]
        ntot = weight.shape[1]
        gld = {}
        w2ld = {}
        mmd = {}
        add_cnt = {}
        xost = {}
        xcb = {}
        store_ready = [None] * NS

        def emit_xc_loads(dg):
            for tau in range(NS):
                self._wait_fr("y", self.fr["XC8"][tau])
                if src_ready is not None and src_ready[tau] is not None:
                    self.w("y", src_ready[tau][0], src_ready[tau][1])
                xcb[dg] = self.dma(
                    "y", self.XCs[dg % 2], self.XT[:, tau * 512:(tau + 1) * 512],
                    bass.AP(xsrc, tau * 128 * D + dg * 512, [[D, 128], [1, 512]]))

        for dg in range(DG):
            for cc in range(nchunks):
                p = cc % 3
                p4 = cc % 3
                if cc == min(2, nchunks - 1):
                    emit_xc_loads(dg)
                if stream_g:
                    self._wait_fr("y", self.fr["GS"][p])
                    if dg == 0:
                        for sm, vv in gready(cc):
                            self.w("y", sm, vv)
                    gld[cc] = self.dma("y", self.GSs[p], self.GS[:, p * T:(p + 1) * T],
                                       bass.AP(gsrc, cc * 128 * T, [[T, 128], [1, T]]))
                if wbf is not None:
                    self._wait_fr("y", self.fr["W2S"][p4])
                    if dg == 0 and cc == 0:
                        self.w("y", self.W2C, self.w2c_total)
                    w2ld[cc] = self.dma("y", self.W2m[p4],
                                        self.W2S[:, p4 * 512:(p4 + 1) * 512],
                                        bass.AP(wbf, cc * 128 * ntot + dg * 512,
                                                [[ntot, 128], [1, 512]]))
                else:
                    self._wait_fr("g", self.fr["W2S"][p4])
                    w2ld[cc] = self.dma("g", self.W2g[p4],
                                        self.W2S[:, p4 * 512:(p4 + 1) * 512],
                                        bass.AP(weight, cc * 128 * ntot + dg * 512,
                                                [[ntot, 128], [1, 512]]))
                if stream_g:
                    self.w("t", self.GSs[p], gld[cc])
                self.w("t", self.W2m[p4] if wbf is not None else self.W2g[p4], w2ld[cc])
                if cc == 0:
                    for tau in range(NS):
                        self._wait_fr("t", self.bank_free[tau])
                    if not stream_g:
                        self.w("t", self.V, self.afm_ready)
                for tau in range(NS):
                    self.o("t", "matmul", self.PS[tau][:, 0:512],
                           lhs_of_chunk(cc, tau), self.W2S[:, p4 * 512:(p4 + 1) * 512],
                           start=(cc == 0),
                           stop=(not lora) and (cc == nchunks - 1),
                           inc=(self.T if tau == NS - 1 else None))
                mmd[cc] = self.T.n
                self.fr["W2S"][p4] = (self.T, mmd[cc])
                if stream_g:
                    self.fr["GS"][p] = (self.T, mmd[cc])
            taud = {}
            if lora:
                self.w("t", self.S, self.s2_ready)
                for tau in range(NS):
                    taud[tau] = self.o("t", "matmul", self.PS[tau][:, 0:512],
                                       self.S2B[0:16, tau * 128:(tau + 1) * 128],
                                       self.B2S[0:16, dg * 512:(dg + 1) * 512],
                                       start=False, stop=True, inc=self.T)
            else:
                for tau in range(NS):
                    taud[tau] = mmd[nchunks - 1]
            for tau in range(NS):
                p = tau % 2
                self.w("v", self.T, taud[tau])
                self.w("v", self.XCs[dg % 2], xcb[dg])
                self._wait_fr("v", self.fr["XO"][p])
                add_cnt[(dg, tau)] = self.o(
                    "v", "tensor_add", self.XO[:, p * 512:(p + 1) * 512],
                    self.PS[tau][:, 0:512], self.XT[:, tau * 512:(tau + 1) * 512],
                    inc=self.V)
                self.bank_free[tau] = (self.V, add_cnt[(dg, tau)])
                self.fr["XC8"][tau] = (self.V, add_cnt[(dg, tau)])
                self.w("y", self.V, add_cnt[(dg, tau)])
                xost[(dg, tau)] = self.dma(
                    "y", self.XOs[p],
                    bass.AP(xdst, tau * 128 * D + dg * 512, [[D, 128], [1, 512]]),
                    self.XO[:, p * 512:(p + 1) * 512])
                self.fr["XO"][p] = (self.XOs[p], xost[(dg, tau)])
        for tau in range(NS):
            store_ready[tau] = (self.XOs[tau % 2], xost[(DG - 1, tau)])
        self.fr["XT"] = [(self.V, add_cnt[(DG - 1, NS - 1)])] * c["NHALF"]
        if stream_g and gsrc is self.gbuf:
            self.gbuf_read_done = [(s, s.n) for s in self.GSs]
        if final:
            self.w("y", self.XOs[0], self.XOs[0].n)
            self.w("y", self.XOs[1], self.XOs[1].n)
        return store_ready

    def phase_proj(self, group, interleave_core=False):
        c = self.c
        DC, NH, NS, T, W, TH, D = c["DC"], c["NH"], c["NS"], c["T"], c["W"], c["TH"], c["D"]
        HID = c["HID"]
        if interleave_core:
            blocks = []
            for h in range(NH):
                blocks += [("k", h), ("v", h)]
        else:
            blocks = [("k", h) for h in range(NH)] + [("v", h) for h in range(NH)]
            if group == 0:
                blocks += [("q", h) for h in range(NH)]
        wld = {}
        mm_done = {}

        CW = DC * 128

        def issue_loads(bi):
            kind, h = blocks[bi]
            p = bi % 2
            self._wait_fr("g", self.fr["WS1"][p])
            mat, blk = {
                "k": (self.wkv, h),
                "v": (self.wkv, NH + h),
                "q": (self.wq, h),
            }[kind]
            wld[bi] = self.dma(
                "g", self.WSm[p], self.WS1[:, p * CW:(p + 1) * CW],
                bass.AP(mat, blk * 128 * CW, [[CW, 128], [1, CW]]))

        issue_loads(0)
        issue_loads(1)
        nb = 0
        for bi, (kind, h) in enumerate(blocks):
            p = bi % 2
            self.w("t", self.WSm[p], wld[bi])
            if bi == 0:
                self.w("t", self.V, self.xln_ready)
            banks = [(nb + w) % 4 for w in range(TH)]
            nb += TH
            for w in range(TH):
                bk = banks[w]
                self._wait_fr("t", self.bank_free[bk])
                for dc in range(DC):
                    self.o("t", "matmul", self.PS[bk][:, 0:W],
                           self.WS1[:, p * DC * 128 + dc * 128: p * DC * 128 + (dc + 1) * 128],
                           self.arena[:, dc * T + w * W: dc * T + (w + 1) * W],
                           start=(dc == 0), stop=(dc == DC - 1),
                           inc=(self.T if dc == DC - 1 else None))
            mm_done[bi] = self.T.n
            self.fr["WS1"][p] = (self.T, mm_done[bi])
            if bi + 2 < len(blocks):
                issue_loads(bi + 2)
            self._wait_fr("s", self.fr["SIL"][p])
            evs = []
            for w in range(TH):
                bk = banks[w]
                self.w("s", self.T, mm_done[bi] - (TH - 1 - w))
                kwq = {"scale": SCALE_Q} if kind == "q" else {}
                e = self.o("s", "activation", self.SIL[:, p * T + w * W: p * T + (w + 1) * W],
                           self.PS[bk][:, 0:W], AF.Copy, inc=self.S, **kwq)
                self.bank_free[bk] = (self.S, e)
                evs.append(e)
            ev = evs[-1]
            if kind == "k":
                self.w("y", self.S, ev)
                stc = self.dma("y", self.SILs[p],
                               bass.AP(self.kloc, h * 128 * 2 * T + group * T,
                                       [[2 * T, 128], [1, T]]),
                               self.SIL[:, p * T:(p + 1) * T])
                self.kvready.setdefault((group, h), []).append((self.SILs[p], stc))
                self.fr["SIL"][p] = (self.SILs[p], stc)
            elif kind == "q":
                self.w("y", self.S, ev)
                stc = self.dma("y", self.SILs[p],
                               bass.AP(self.qloc, h * 128 * T, [[T, 128], [1, T]]),
                               self.SIL[:, p * T:(p + 1) * T])
                self.qready[h] = (self.SILs[p], stc)
                self.fr["SIL"][p] = (self.SILs[p], stc)
            else:
                bkT = 4 + (bi % 2)
                self._wait_fr("t", self.bank_free[bkT])
                self.w("t", self.S, ev)
                for tau in range(NS):
                    self.o("t", "transpose", self.PSb[bkT][:, (tau % 8) * 128:(tau % 8) * 128 + 128],
                           self.SIL[:, p * T + tau * 128: p * T + (tau + 1) * 128],
                           self.IDN[:, :],
                           inc=(self.T if (tau % 8 == 7 or tau == NS - 1) else None))
                    if tau % 8 == 7 or tau == NS - 1:
                        base = (tau // 8) * 8
                        cnt = tau - base + 1
                        self.w("v", self.T, self.T.n)
                        self._wait_fr("v", self.fr["H3B"][p])
                        e2 = self.o("v", "tensor_copy",
                                    self.H3B[:, p * T + base * 128: p * T + (base + cnt) * 128],
                                    self.PSb[bkT][:, 0:cnt * 128], inc=self.V)
                        self.bank_free[bkT] = (self.V, e2)
                tr_done = self.T.n
                self.fr["SIL"][p] = (self.T, tr_done)
                self.w("y", self.V, self.V.n)
                stc = self.dma("y", self.VSTs[p],
                               bass.AP(self.vloc, h * 2 * T * 128 + group * T * 128,
                                       [[128, 128], [128 * 128, NS], [1, 128]]),
                               self.H3B[:, p * T:(p + 1) * T])
                self.kvready.setdefault((group, h), []).append((self.VSTs[p], stc))
                self.fr["H3B"][p] = (self.VSTs[p], stc)
            if interleave_core and kind == "v":
                self._core_loads(h)
                if h >= 1:
                    self._core_compute(h - 1)
        if interleave_core:
            self._core_compute(NH - 1)
        self.proj_done = self.T.n

    def core_init(self):
        self._sc_bank = 0
        self._pt_bank = 0
        self._kvh = {}
        self.afm_ready_h = {}

    def _core_loads(self, h):
        c = self.c
        NS, T = c["NS"], c["T"]
        p = h % 2
        self._wait_fr("y", self.fr["QH"][p])
        qs, qv = self.qready[h]
        self.w("y", qs, qv)
        for sm, vv in self.kvready[(0, h)] + self.kvready[(1, h)]:
            self.w("y", sm, vv)
        self.dma("y", self.KV[p], self.QH[:, p * T:(p + 1) * T],
                 bass.AP(self.qloc, h * 128 * T, [[T, 128], [1, T]]))
        self._wait_fr("y", self.fr["KH"][p])
        self.dma("y", self.KV[p], self.KH[:, p * 2 * T:(p + 1) * 2 * T],
                 bass.AP(self.kloc, h * 128 * 2 * T, [[2 * T, 128], [1, 2 * T]]))
        self._wait_fr("y", self.fr["VH"][p])
        self._kvh[h] = self.dma(
            "y", self.KV[p],
            self.VH[:, p * 2 * T:(p + 1) * 2 * T],
            bass.AP(self.vloc, h * 2 * T * 128,
                    [[128, 128], [128 * 128, 2 * NS], [1, 128]]))

    def _core_compute(self, h):
        c = self.c
        NH, NS, T = c["NH"], c["NS"], c["T"]
        av_done = {}
        sc_bank = 0
        if True:
            p = h % 2
            vh = self._kvh
            self.w("t", self.KV[p], vh[h])
            for j in range(NS):
                pp = (h * NS + j) % 2
                L = 128 * (j + 1)
                WJ = 2 * L
                ncch = (WJ + 511) // 512
                SB = pp * 24
                chunk_banks = []
                chunk_T = []
                for ch in range(ncch):
                    c0 = ch * 512
                    cw = min(512, WJ - c0)
                    bk = self._sc_bank % 4
                    self._sc_bank += 1
                    self._wait_fr("t", self.bank_free[bk])
                    frs = []
                    if c0 < L:
                        e = min(L, c0 + cw)
                        frs.append((c0, e, c0))
                    if c0 + cw > L:
                        b = max(L, c0)
                        frs.append((b, c0 + cw, T + (b - L)))
                    for fi, (a, b, src) in enumerate(frs):
                        self.o("t", "matmul", self.PS[bk][:, a - c0: b - c0],
                               self.QH[:, p * T + j * 128: p * T + (j + 1) * 128],
                               self.KH[:, p * 2 * T + src: p * 2 * T + src + (b - a)],
                               start=(fi == 0), stop=(fi == len(frs) - 1),
                               inc=(self.T if fi == len(frs) - 1 else None))
                    chunk_banks.append((bk, c0, cw))
                    chunk_T.append(self.T.n)
                mk_v = {}
                for mi, pos in enumerate((128 * j, L + 128 * j)):
                    ch = pos // 512
                    bk, c0, cw = chunk_banks[ch]
                    self.w("v", self.T, chunk_T[ch])
                    mk_v[ch] = self.o("v", "tensor_tensor",
                                      self.PS[bk][:, pos - c0: pos - c0 + 128],
                                      self.PS[bk][:, pos - c0: pos - c0 + 128],
                                      self.MSKS[:, j * 256 + mi * 128: j * 256 + (mi + 1) * 128],
                                      ALU.add, inc=self.V)
                exp_S = []
                for ch, (bk, c0, cw) in enumerate(chunk_banks):
                    self.w("v", self.T, chunk_T[ch])
                    if ch in mk_v:
                        self.w("v", self.V, mk_v[ch])
                    mx = self.o("v", "reduce_max", self.SFX[:, SB + ch: SB + ch + 1],
                                self.PS[bk][:, 0:cw], AX.X, negate=True, inc=self.V)
                    self.w("s", self.V, mx)
                    if ch == 0:
                        self._wait_fr("s", self.fr["PT"][pp])
                    e = self.o("s", "activation", self.PT[:, pp * 2 * T + c0: pp * 2 * T + c0 + cw],
                               self.PS[bk][:, 0:cw], AF.Exp,
                               bias=self.SFX[:, SB + ch: SB + ch + 1],
                               accum_out=self.SFX[:, SB + 8 + ch: SB + 9 + ch], inc=self.S)
                    exp_S.append(e)
                    self.bank_free[bk] = (self.S, e)
                self.w("v", self.S, exp_S[-1])
                self.o("v", "tensor_reduce", self.SFX[:, SB + 16: SB + 17],
                       self.SFX[:, SB: SB + ncch], AX.X, ALU.min, inc=self.V)
                self.w("s", self.V, self.V.n)
                dl = self.o("s", "activation", self.SFX[:, SB + 4: SB + 4 + ncch],
                            self.SFX[:, SB: SB + ncch], AF.Exp,
                            bias=self.SFX[:, SB + 16: SB + 17], scale=-1.0, inc=self.S)
                self.w("v", self.S, dl)
                self.o("v", "tensor_mul", self.SFX[:, SB + 12: SB + 12 + ncch],
                       self.SFX[:, SB + 4: SB + 4 + ncch],
                       self.SFX[:, SB + 8: SB + 8 + ncch], inc=self.V)
                self.w("v", self.V, self.V.n)
                self.o("v", "reduce_sum", self.SFX[:, SB + 17: SB + 18],
                       self.SFX[:, SB + 12: SB + 12 + ncch], AX.X, inc=self.V)
                self.w("v", self.V, self.V.n)
                self.o("v", "reciprocal", self.SFX[:, SB + 18: SB + 19],
                       self.SFX[:, SB + 17: SB + 18], inc=self.V)
                self.w("v", self.V, self.V.n)
                sc = self.o("v", "tensor_scalar", self.SFX[:, SB + 4: SB + 4 + ncch],
                            self.SFX[:, SB + 4: SB + 4 + ncch],
                            self.SFX[:, SB + 18: SB + 19], None, ALU.mult, inc=self.V)
                pn = []
                for ch, (bk, c0, cw) in enumerate(chunk_banks):
                    self.w("v", self.V, sc)
                    e = self.o("v", "tensor_scalar",
                               self.PT[:, pp * 2 * T + c0: pp * 2 * T + c0 + cw],
                               self.PT[:, pp * 2 * T + c0: pp * 2 * T + c0 + cw],
                               self.SFX[:, SB + 4 + ch: SB + 5 + ch], None, ALU.mult,
                               inc=self.V)
                    pn.append(e)
                nkt = 2 * (j + 1)
                tev = []
                i = 0
                while i < nkt:
                    nb2 = min(8, nkt - i)
                    bkT = 4 + (self._pt_bank % 2)
                    self._pt_bank += 1
                    self._wait_fr("t", self.bank_free[bkT])
                    self.w("t", self.V, pn[-1])
                    for q in range(nb2):
                        kt = i + q
                        self.o("t", "transpose", self.PSb[bkT][:, q * 128:(q + 1) * 128],
                               self.PT[:, pp * 2 * T + kt * 128: pp * 2 * T + (kt + 1) * 128],
                               self.IDN[:, :], inc=(self.T if q == nb2 - 1 else None))
                    tcnt = self.T.n
                    if i == 0:
                        self._wait_fr("v", self.fr["PF"][pp])
                    self.w("v", self.T, tcnt)
                    e = self.o("v", "tensor_copy",
                               self.PF[:, pp * 2 * T + i * 128: pp * 2 * T + (i + nb2) * 128],
                               self.PSb[bkT][:, 0:nb2 * 128], inc=self.V)
                    tev.append(e)
                    self.bank_free[bkT] = (self.V, e)
                    i += nb2
                self.fr["PT"][pp] = (self.T, self.T.n)
                g4 = min(4, NS)
                bkA = 6 + ((j // g4) % 2)
                col = (j % g4) * 128
                if j % g4 == 0:
                    self._wait_fr("t", self.bank_free[bkA])
                self.w("t", self.V, tev[-1])
                self.w("t", self.KV[p], vh[h])
                for kt in range(nkt):
                    r = 0 if kt <= j else 1
                    sslot = kt if kt <= j else kt - (j + 1)
                    vcol = p * 2 * T + (r * NS + sslot) * 128
                    self.o("t", "matmul", self.PS[bkA][:, col: col + 128],
                           self.VH[:, vcol: vcol + 128],
                           self.PF[:, pp * 2 * T + kt * 128: pp * 2 * T + (kt + 1) * 128],
                           start=(kt == 0), stop=(kt == nkt - 1),
                           inc=(self.T if kt == nkt - 1 else None))
                av_done[(h, j)] = self.T.n
                self.fr["PF"][pp] = (self.T, av_done[(h, j)])
                if j % g4 == g4 - 1 or j == NS - 1:
                    jb = (j // g4) * g4
                    ncol = (j - jb + 1) * 128
                    par = (h * ((NS + g4 - 1) // g4) + j // g4) % 2
                    self.w("v", self.T, av_done[(h, j)])
                    self._wait_fr("v", self.fr["AFS"][par])
                    e = self.o("v", "tensor_copy",
                               self.AFS[:, par * 512: par * 512 + ncol],
                               self.PS[bkA][:, 0:ncol], inc=self.V)
                    self.bank_free[bkA] = (self.V, e)
                    self.w("y", self.V, e)
                    stc = self.dma("y", self.AFSt[par],
                                   bass.AP(self.afm, h * 128 * T + jb * 128,
                                           [[T, 128], [1, ncol]]),
                                   self.AFS[:, par * 512: par * 512 + ncol])
                    self.fr["AFS"][par] = (self.AFSt[par], stc)
                    self.afm_ready_h.setdefault(h, []).append((self.AFSt[par], stc))
            self.fr["QH"][p] = (self.T, self.T.n)
            self.fr["KH"][p] = (self.T, self.T.n)
            self.fr["VH"][p] = (self.T, self.T.n)

    def finalize(self):
        nc = self.nc
        for s in self.sems:
            s.h = nc.alloc_semaphore(s.name)
        with nc.Block() as blk:
            blk.tensor(lambda E, f=self.prog["t"]: [fn(E) for fn in f])
            blk.vector(lambda E, f=self.prog["v"]: [fn(E) for fn in f])
            blk.scalar(lambda E, f=self.prog["s"]: [fn(E) for fn in f])
            blk.sync(lambda E, f=self.prog["y"]: [fn(E) for fn in f])
            blk.gpsimd(lambda E, f=self.prog["g"]: [fn(E) for fn in f])
        return nc


def build_program(cfg):
    c = _derived(cfg)
    c["NCORES"] = cfg["NCORES"]
    nc = bass.Bass("TRN2", target_bir_lowering=False)
    b = _B(nc, c)
    b.kvready = {}
    b.qready = {}
    b.prelude()
    gs_lhs = lambda cc, tau: b.GS[:, (cc % 3) * c["T"] + tau * 128: (cc % 3) * c["T"] + (tau + 1) * 128]
    gb_ready = lambda cc: [(b.GST[cc % 2], b.gst[cc])]
    b.phase_a(b.x0, None)
    b.phase_b(convert=True)
    r1 = b.phase_c(b.x0, b.xr1, b.w2t, c["HCN"], gs_lhs, True, True, None,
                   wbf=b.w2bf, gsrc=b.gbuf, gready=gb_ready)
    b.phase_a(b.x0p, None)
    b.phase_b()
    r1p = b.phase_c(b.x0p, b.xr1p, b.w2t, c["HCN"], gs_lhs, True, True, None,
                    wbf=b.w2bf, gsrc=b.gbuf, gready=gb_ready)
    b.phase_a(b.xr1, r1)
    b.phase_proj(0)
    b.phase_a(b.xr1p, r1p)
    b.core_init()
    b.phase_proj(1, interleave_core=True)
    r2 = b.phase_c(b.xr1, b.xr2, b.wo, c["NH"], gs_lhs, True, False, r1,
                   gsrc=b.afm, gready=lambda cc: b.afm_ready_h[cc])
    b.phase_a(b.xr2, r2)
    b.phase_b()
    b.phase_c(b.xr2, b.out, b.w2t, c["HCN"], gs_lhs, True, True, r2, final=True,
              wbf=b.w2bf, gsrc=b.gbuf, gready=gb_ready)
    return b.finalize()



def own_tile(r, j):
    return 2 * j + ((j & 1) ^ r)


def build_masks(cfg, r):
    c = _derived(cfg)
    NS = c["NS"]
    msk = np.zeros((NS * 128, 256), dtype=ml_dtypes.bfloat16)
    tri = np.where(np.arange(128)[None, :] <= np.arange(128)[:, None], 0.0, NEG
                   ).astype(ml_dtypes.bfloat16)
    for j in range(NS):
        g_self = own_tile(r, j)
        g_sib = own_tile(1 - r, j)
        msk[j * 128:(j + 1) * 128, 0:128] = tri
        if g_sib > g_self:
            msk[j * 128:(j + 1) * 128, 128:256] = NEG
    return msk


def host_layout(cfg, canon, ncores):
    c = _derived(cfg)
    NS, D, T = c["NS"], c["D"], c["T"]
    x = np.ascontiguousarray(canon["x"], dtype=np.float32)
    B, S, _ = x.shape
    assert S == 2 * T and B == ncores // 2
    ident = np.eye(128, dtype=np.float32)
    DC, HCN, NH = c["DC"], c["HCN"], c["NH"]

    def chunk_major(w, nblk):
        return np.ascontiguousarray(
            np.asarray(w).reshape(DC, 128, nblk, 128).transpose(2, 1, 0, 3)
            .reshape(nblk * 128, DC * 128))

    shared = {
        "w1t": chunk_major(canon["w1t"], HCN),
        "w3t": chunk_major(canon["w3t"], HCN),
        "w2t": canon["w2t"],
        "wq": chunk_major(canon["wq"], NH),
        "wkv": chunk_major(canon["wkv"], 2 * NH),
        "wo": canon["wo"],
        "a13": canon["a13"], "b13": canon["b13"], "a2": canon["a2t"],
        "b2": canon["b2t"], "idn": ident,
    }
    shared = {k: np.ascontiguousarray(v, dtype=np.float32) for k, v in shared.items()}
    in_maps = []
    for core in range(ncores):
        bidx, r = core // 2, core % 2
        tiles = [own_tile(r, j) for j in range(NS)]
        ptiles = [own_tile(1 - r, j) for j in range(NS)]
        xr = x[bidx].reshape(2 * NS, 128, D)
        m = dict(shared)
        m["x0"] = np.ascontiguousarray(xr[tiles].reshape(T, D))
        m["x0p"] = np.ascontiguousarray(xr[ptiles].reshape(T, D))
        m["msk"] = build_masks(cfg, r)
        in_maps.append(m)
    return in_maps


def assemble_out(cfg, outs, ncores):
    c = _derived(cfg)
    NS, D, T = c["NS"], c["D"], c["T"]
    B = ncores // 2
    full = np.zeros((B, 2 * NS, 128, D), dtype=np.float32)
    for core in range(ncores):
        bidx, r = core // 2, core % 2
        o = outs[core].reshape(NS, 128, D)
        for j in range(NS):
            full[bidx, own_tile(r, j)] = o[j]
    return full.reshape(B, 2 * NS * 128, D)


def canonicalize(inputs):
    H, DN, DR, DV = 32, 128, 64, 128
    DQ = DN + DR
    wq = inputs["L0_attn_wq"]
    D = wq.shape[0]
    wq_used = np.ascontiguousarray(
        np.asarray(wq).reshape(D, H, DQ)[:, :, :DN].reshape(D, H * DN))
    wkv = np.asarray(inputs["L0_attn_wkv"])
    wkv_used = np.ascontiguousarray(wkv[:, :H * (DN + DV)])
    w1 = np.asarray(inputs["L0_ffn_w1"])
    w3 = np.asarray(inputs["L0_ffn_w3"])
    w2 = np.asarray(inputs["L0_ffn_w2"])
    a1 = np.asarray(inputs["L0_ffn_lora_A1"])
    a3 = np.asarray(inputs["L0_ffn_lora_A3"])
    b1 = np.asarray(inputs["L0_ffn_lora_B1"])
    b3 = np.asarray(inputs["L0_ffn_lora_B3"])
    a2 = np.asarray(inputs["L0_ffn_lora_A2"])
    b2 = np.asarray(inputs["L0_ffn_lora_B2"])
    canon = dict(
        x=np.asarray(inputs["L0_attn_hidden_states"]),
        w1t=np.ascontiguousarray(w1.T),
        w3t=np.ascontiguousarray(w3.T),
        w2t=np.ascontiguousarray(w2.T),
        a13=np.ascontiguousarray(np.concatenate(
            [a1.T.reshape(D, 16), a3.T.reshape(D, 16)], axis=1)
            .reshape(D, 32)),
        b13=np.ascontiguousarray(np.concatenate([b1.T, b3.T], axis=1)),
        a2t=np.ascontiguousarray(a2.T),
        b2t=np.ascontiguousarray(b2.T),
        wq=wq_used,
        wkv=wkv_used,
        wo=np.ascontiguousarray(np.asarray(inputs["L0_attn_wo"])),
    )
    return canon


def kernel(**inputs):
    cfg = full_cfg()
    canon = canonicalize(inputs)
    in_maps = host_layout(cfg, canon, cfg["NCORES"])
    nc = build_program(cfg)
    res = run_bass_kernel_spmd(nc, in_maps, core_ids=list(range(cfg["NCORES"])))
    outs = [res.results[i]["out"] for i in range(cfg["NCORES"])]
    return assemble_out(cfg, outs, cfg["NCORES"]).astype(np.float32)


# version 08
# speedup vs baseline: 1.1493x; 1.1493x over previous
import numpy as np
import ml_dtypes
import concourse.bass as bass
import concourse.mybir as mybir
from concourse.bass_utils import run_bass_kernel_spmd

F32 = mybir.dt.float32
BF16 = mybir.dt.bfloat16
AF = mybir.ActivationFunctionType
ALU = mybir.AluOpType
AX = mybir.AxisListType

NEG = -1.0e30
SCALE_Q = float(192.0 ** -0.5)
LN_EPS = 1.0e-5


def full_cfg():
    return dict(D=4096, HID=11008, NH=32, NS=8, NCORES=8)


def _derived(cfg):
    D, HID, NH, NS = cfg["D"], cfg["HID"], cfg["NH"], cfg["NS"]
    T = NS * 128
    W = min(512, T)
    d = dict(
        D=D, HID=HID, NH=NH, NS=NS, T=T, W=W,
        TH=T // W,
        DC=D // 128,
        HCN=HID // 128,
        DG=D // 512,
        HW=min(D, 2048),
    )
    d["NHALF"] = D // d["HW"]
    return d


class _Sem:
    __slots__ = ("name", "n", "h")

    def __init__(self, name):
        self.name = name
        self.n = 0
        self.h = None


class _B:

    def __init__(self, nc, c):
        self.nc = nc
        self.c = c
        self.prog = {"t": [], "v": [], "s": [], "y": [], "g": []}
        self.PRE = _Sem("PRE")
        self.MSK = _Sem("MSK")
        self.sXT = _Sem("sXT")
        self.GST = [_Sem("GST0"), _Sem("GST1")]
        self.GSs = [_Sem("GSs0"), _Sem("GSs1"), _Sem("GSs2")]
        self.XCs = [_Sem("XCs0"), _Sem("XCs1")]
        self.XOs = [_Sem("XOs0"), _Sem("XOs1")]
        self.SILs = [_Sem("SILs0"), _Sem("SILs1")]
        self.VSTs = [_Sem("VSTs0"), _Sem("VSTs1")]
        self.KV = [_Sem("KV0"), _Sem("KV1")]
        self.AFSt = [_Sem("AFSt0"), _Sem("AFSt1")]
        self.WSm = [_Sem("WSm0"), _Sem("WSm1")]
        self.WB = [_Sem("WB0"), _Sem("WB1")]
        self.WSy = [_Sem("WSy0"), _Sem("WSy1")]
        self.W2g = [_Sem(f"W2g{i}") for i in range(3)]
        self.B13m = [_Sem("B13m0"), _Sem("B13m1")]
        self.W2C = _Sem("W2C")
        self.W2m = [_Sem(f"W2m{i}") for i in range(3)]
        self.T = _Sem("T")
        self.V = _Sem("V")
        self.S = _Sem("S")
        self.sems = ([self.PRE, self.MSK, self.sXT,
                      self.T, self.V, self.S]
                     + self.GST + self.GSs + self.XCs + self.XOs
                     + self.SILs + self.VSTs + self.KV + self.AFSt + self.WSm + self.W2m
                     + self.WB + self.WSy + self.B13m + self.W2g + [self.W2C])
        self._declare()
        self.bank_free = [(self.V, 0) for _ in range(8)]

    def w(self, e, sem, v):
        if v > 0:
            self.prog[e].append(lambda E, s=sem, vv=v: E.wait_ge(s.h, vv))

    def o(self, e, meth, *a, inc=None, k=1, **kw):
        if inc is not None:
            inc.n += k
        v = inc.n if inc is not None else 0

        def f(E, m=meth, aa=a, kk=kw, s=inc, n=k):
            r = getattr(E, m)(*aa, **kk)
            if s is not None:
                r.then_inc(s.h, n)

        self.prog[e].append(f)
        return v

    def dma(self, e, sem, dst, src, **kw):
        return self.o(e, "dma_start", inc=sem, k=16, out=dst, in_=src, **kw)

    def _declare(self):
        nc, c = self.nc, self.c
        D, HID, NH, NS, T = c["D"], c["HID"], c["NH"], c["NS"], c["T"]
        DC, HCN = c["DC"], c["HCN"]
        dt = nc.dram_tensor
        self.x0 = dt("x0", [T, D], F32, kind="ExternalInput")
        self.x0p = dt("x0p", [T, D], F32, kind="ExternalInput")
        self.w1t = dt("w1t", [HCN * 128, DC * 128], F32, kind="ExternalInput")
        self.w3t = dt("w3t", [HCN * 128, DC * 128], F32, kind="ExternalInput")
        self.w2t = dt("w2t", [HID, D], F32, kind="ExternalInput")
        self.wq = dt("wq", [NH * 128, DC * 128], F32, kind="ExternalInput")
        self.wkv = dt("wkv", [2 * NH * 128, DC * 128], F32, kind="ExternalInput")
        self.wo = dt("wo", [NH * 128, D], F32, kind="ExternalInput")
        self.a13 = dt("a13", [D, 32], F32, kind="ExternalInput")
        self.b13 = dt("b13", [16, 2 * HID], F32, kind="ExternalInput")
        self.a2 = dt("a2", [HID, 16], F32, kind="ExternalInput")
        self.b2 = dt("b2", [16, D], F32, kind="ExternalInput")
        self.msk = dt("msk", [NS * 128, 256], BF16, kind="ExternalInput")
        self.idn = dt("idn", [128, 128], F32, kind="ExternalInput")
        self.out = dt("out", [T, D], F32, kind="ExternalOutput")
        self.xr1 = dt("xr1", [T, D], F32)
        self.xr1p = dt("xr1p", [T, D], F32)
        self.xr2 = dt("xr2", [T, D], F32)
        self.gbuf = dt("gbuf", [HCN * 128, T], BF16)
        self.qloc = dt("qloc", [NH * 128, T], BF16)
        self.kloc = dt("kloc", [NH * 128, 2 * T], BF16)
        self.vloc = dt("vloc", [NH * 2 * T, 128], BF16)
        self.afm = dt("afm", [NH * 128, T], BF16)
        self.w1bf = dt("w1bf", [HCN * 128, DC * 128], BF16)
        self.w3bf = dt("w3bf", [HCN * 128, DC * 128], BF16)
        self.w2bf = dt("w2bf", [HID, D], BF16)

        st = nc.alloc_sbuf_tensor
        ACOL = max(DC, NH) * T
        self.arena = st("arena", [128, ACOL], BF16)
        self.WS1 = st("WS1", [128, 2 * DC * 128], BF16)
        self.WS3 = st("WS3", [128, 2 * DC * 128], BF16)
        self.W2S = st("W2S", [128, 3 * 512], BF16)
        self.GB = st("GB", [128, 2 * T], BF16)
        self.GS = st("GS", [128, 3 * T], BF16)
        self.SIL = st("SIL", [128, 2 * T], BF16)
        self.H3B = st("H3B", [128, 2 * T], BF16)
        self.XT = st("XT", [128, max(c["NHALF"] * c["HW"], NS * 512)], F32)
        self.XLT = st("XLT", [128, c["NHALF"] * c["HW"]], BF16)
        self.XO = st("XO", [128, 2 * 512], F32)
        self.A13S = st("A13S", [128, DC * 32], BF16)
        self.A2S = st("A2S", [128, HCN * 16], BF16)
        self.B13S = st("B13S", [16, 2 * 256], BF16)
        self.B2S = st("B2S", [16, D], BF16)
        self.S1B = st("S1B", [16, T], BF16)
        self.S3B = st("S3B", [16, T], BF16)
        self.S2B = st("S2B", [16, T], BF16)
        self.QH = st("QH", [128, 2 * T], BF16)
        self.KH = st("KH", [128, 2 * 2 * T], BF16)
        self.VH = st("VH", [128, 2 * 2 * T], BF16)
        self.PT = st("PT", [128, 2 * 2 * T], BF16)
        self.AFS = st("AFS", [128, 2 * 512], BF16)
        self.PF = st("PF", [128, 2 * 2 * T], BF16)
        self.MSKS = st("MSKS", [128, NS * 256], BF16)
        self.STT = st("STT", [128, 32], F32)
        self.CON = st("CON", [128, 8], F32)
        self.SFX = st("SFX", [128, 48], F32)
        self.IDN = st("IDN", [128, 128], BF16)

        self.PS = [nc.alloc_psum_tensor(f"PS{i}", [128, 512], F32) for i in range(8)]
        self.PSb = [p.ap().bitcast(BF16) for p in self.PS]

        z = lambda: [(self.V, 0), (self.V, 0)]
        self.fr = {k: z() for k in (
            "WS1", "WS3", "GB", "SIL", "H3B", "XO",
            "B13S", "QH", "KH", "VH", "PT", "PF")}
        self.fr["GS"] = [(self.V, 0)] * 3
        self.fr["W2S"] = [(self.V, 0)] * 3
        self.fr["AFS"] = [(self.V, 0), (self.V, 0)]
        self.fr["XT"] = [(self.V, 0)] * c["NHALF"]
        self.fr["XC8"] = [(self.V, 0)] * NS
        self.fr["XLT"] = [(self.V, 0)] * c["NHALF"]

    def prelude(self):
        c = self.c
        DC, HCN, D, NS = c["DC"], c["HCN"], c["D"], c["NS"]
        for i, val in enumerate((LN_EPS, 2.0, 0.5, SCALE_Q, -1.0)):
            ap = self.CON[:, i:i + 1]
            self.o("v", "memset", ap, val,
                   inc=(self.V if i == 4 else None))
            self.nc.const_aps.register(F32, val, ap)
        self.const_V = self.V.n
        self.dma("g", self.PRE, self.IDN[:, :], self.idn[:, :])
        self.dma("g", self.PRE, self.A13S[:, :],
                 bass.AP(self.a13, 0, [[32, 128], [128 * 32, DC], [1, 32]]))
        self.dma("g", self.PRE, self.A2S[:, :],
                 bass.AP(self.a2, 0, [[16, 128], [128 * 16, HCN], [1, 16]]))
        self.dma("g", self.PRE, self.B2S[:, :], self.b2[:, :])
        self.dma("y", self.MSK, self.MSKS[:, :],
                 bass.AP(self.msk, 0, [[256, 128], [128 * 256, NS], [1, 256]]))
        for e in ("t", "s", "v"):
            self.w(e, self.V, self.const_V)
            self.w(e, self.PRE, self.PRE.n)
            self.w(e, self.MSK, self.MSK.n)

    def phase_a(self, xsrc, src_ready):
        c = self.c
        NS, D, HW, NHALF, T = c["NS"], c["D"], c["HW"], c["NHALF"], c["T"]
        nbank = 0
        norm_cnt = {}
        trans_cnt = {}
        for tau in range(NS):
            SB = (tau % 2) * 16
            ld = {}
            for h in range(NHALF):
                self._wait_fr("y", self.fr["XT"][h])
                if src_ready is not None and src_ready[tau] is not None:
                    self.w("y", src_ready[tau][0], src_ready[tau][1])
                ld[h] = self.dma(
                    "y", self.sXT, self.XT[:, h * HW:(h + 1) * HW],
                    bass.AP(xsrc, tau * 128 * D + h * HW, [[D, 128], [1, HW]]))
            ld_last = ld[NHALF - 1]
            for h in range(NHALF):
                self.w("v", self.sXT, ld_last)
                self.o("v", "reduce_sum", self.STT[:, SB + h:SB + h + 1],
                       self.XT[:, h * HW:(h + 1) * HW], AX.X, negate=True, inc=self.V)
                self.w("s", self.sXT, ld_last)
                self._wait_fr("s", self.fr["XLT"][h])
                self.o("s", "activation", self.XLT[:, h * HW:(h + 1) * HW],
                       self.XT[:, h * HW:(h + 1) * HW], AF.Square,
                       accum_out=self.STT[:, SB + 2 + h:SB + 3 + h], inc=self.S)
            sq_done = self.S.n
            self.w("v", self.S, sq_done)
            self.w("v", self.V, self.V.n)
            if NHALF == 2:
                self.o("v", "tensor_add", self.STT[:, SB + 4:SB + 5],
                       self.STT[:, SB:SB + 1], self.STT[:, SB + 1:SB + 2], inc=self.V)
                self.o("v", "tensor_add", self.STT[:, SB + 5:SB + 6],
                       self.STT[:, SB + 2:SB + 3], self.STT[:, SB + 3:SB + 4], inc=self.V)
                nsc, sqc = SB + 4, SB + 5
            else:
                nsc, sqc = SB + 0, SB + 2
            self.w("v", self.V, self.V.n)
            self.o("v", "tensor_scalar_mul", self.STT[:, SB + 6:SB + 7],
                   self.STT[:, nsc:nsc + 1], 1.0 / D, inc=self.V)
            self.o("v", "tensor_scalar_mul", self.STT[:, SB + 7:SB + 8],
                   self.STT[:, sqc:sqc + 1], 1.0 / D, inc=self.V)
            self.w("v", self.V, self.V.n)
            self.o("v", "tensor_mul", self.STT[:, SB + 8:SB + 9],
                   self.STT[:, SB + 6:SB + 7], self.STT[:, SB + 6:SB + 7], inc=self.V)
            self.w("v", self.V, self.V.n)
            self.o("v", "tensor_sub", self.STT[:, SB + 9:SB + 10],
                   self.STT[:, SB + 7:SB + 8], self.STT[:, SB + 8:SB + 9], inc=self.V)
            st1 = self.V.n
            self.w("s", self.V, st1)
            self.o("s", "activation", self.STT[:, SB + 10:SB + 11],
                   self.STT[:, SB + 9:SB + 10], AF.Sqrt, bias=LN_EPS, inc=self.S)
            self.w("v", self.S, self.S.n)
            self.o("v", "reciprocal", self.STT[:, SB + 11:SB + 12],
                   self.STT[:, SB + 10:SB + 11], inc=self.V)
            self.w("v", self.V, self.V.n)
            self.o("v", "tensor_mul", self.STT[:, SB + 12:SB + 13],
                   self.STT[:, SB + 6:SB + 7], self.STT[:, SB + 11:SB + 12], inc=self.V)
            stats_done = self.V.n
            self.w("s", self.V, stats_done)
            for h in range(NHALF):
                norm_cnt[(tau, h)] = self.o(
                    "s", "activation", self.XLT[:, h * HW:(h + 1) * HW],
                    self.XT[:, h * HW:(h + 1) * HW], AF.Identity,
                    bias=self.STT[:, SB + 12:SB + 13],
                    scale=self.STT[:, SB + 11:SB + 12], inc=self.S)
                self.fr["XT"][h] = (self.S, norm_cnt[(tau, h)])
            for h in range(NHALF):
                self.w("t", self.S, norm_cnt[(tau, h)])
                nch = HW // 128
                i = 0
                while i < nch:
                    nb = min(8, nch - i)
                    bk = nbank % 4
                    nbank += 1
                    s, v = self.bank_free[bk]
                    self.w("t", s, v)
                    for q in range(nb):
                        self.o("t", "transpose",
                               self.PSb[bk][:, q * 128:(q + 1) * 128],
                               self.XLT[:, h * HW + (i + q) * 128: h * HW + (i + q + 1) * 128],
                               self.IDN[:, :],
                               inc=(self.T if q == nb - 1 else None))
                    tcnt = self.T.n
                    dcg0 = h * (HW // 128) + i
                    self.w("v", self.T, tcnt)
                    ev = self.o("v", "tensor_copy",
                                bass.AP(self.arena, dcg0 * T + tau * 128,
                                        [[self.arena.shape[1], 128], [T, nb], [1, 128]]),
                                self.PSb[bk][:, 0:nb * 128], inc=self.V)
                    self.bank_free[bk] = (self.V, ev)
                    i += nb
                trans_cnt[(tau, h)] = self.T.n
                self.fr["XLT"][h] = (self.T, trans_cnt[(tau, h)])
        self.xln_ready = self.V.n
        last_norm = max(norm_cnt.values())
        self.fr["XC8"] = [(self.S, last_norm)] * NS

    def phase_b(self, convert=False):
        c = self.c
        DC, HCN, T, W, TH = c["DC"], c["HCN"], c["T"], c["W"], c["TH"]
        HID = c["HID"]
        CW = DC * 128
        s_ev = {}
        for (sbuf, colofs) in ((self.S1B, 0), (self.S3B, 16)):
            for w in range(TH):
                bk = 6 + w
                s, v = self.bank_free[bk]
                self.w("t", s, v)
                self.w("t", self.V, self.xln_ready)
                if colofs == 16:
                    self.w("t", self.S, s_ev[(0, w)])
                for dc in range(DC):
                    self.o("t", "matmul", self.PS[bk][0:16, 0:W],
                           self.A13S[:, dc * 32 + colofs: dc * 32 + colofs + 16],
                           self.arena[:, dc * T + w * W: dc * T + (w + 1) * W],
                           start=(dc == 0), stop=(dc == DC - 1),
                           inc=(self.T if dc == DC - 1 else None))
                self.w("s", self.T, self.T.n)
                s_ev[(0 if colofs == 0 else 1, w)] = self.o(
                    "s", "activation", sbuf[:, w * W:(w + 1) * W],
                    self.PS[bk][0:16, 0:W], AF.Copy, scale=2.0, inc=self.S)
                self.bank_free[bk] = (self.S, s_ev[(0 if colofs == 0 else 1, w)])
        s13_done = self.S.n

        h_mm = {}
        mul_cnt = {}
        gst = {}
        s2_mm = {}
        wld = {}
        wbst = {}

        def issue_loads(cc):
            p = cc % 2
            q = "g" if convert else "y"
            wsem = self.WSm[p] if convert else self.WSy[p]
            wv = 0
            for (srcF, srcB, buf, frk) in ((self.w1t, self.w1bf, self.WS1, "WS1"),
                                           (self.w3t, self.w3bf, self.WS3, "WS3")):
                self._wait_fr(q, self.fr[frk][p])
                if convert:
                    wv = self.dma("g", wsem, buf[:, p * CW:(p + 1) * CW],
                                  bass.AP(srcF, cc * 128 * CW, [[CW, 128], [1, CW]]))
                else:
                    wv = self.dma("y", wsem, buf[:, p * CW:(p + 1) * CW],
                                  bass.AP(srcB, cc * 128 * CW, [[CW, 128], [1, CW]]))
            self._wait_fr("g", self.fr["B13S"][p])
            bv = self.dma(
                "g", self.B13m[p], self.B13S[:, p * 256:(p + 1) * 256],
                bass.AP(self.b13, cc * 128, [[2 * HID, 16], [HID, 2], [1, 128]]))
            wld[cc] = (wsem, wv, self.B13m[p], bv)
            if convert:
                self.w("y", wld[cc][0], wld[cc][1])
                self.w("y", wld[cc][2], wld[cc][3])
                for (srcB, buf) in ((self.w1bf, self.WS1), (self.w3bf, self.WS3)):
                    wbst[cc] = self.dma(
                        "y", self.WB[p],
                        bass.AP(srcB, cc * 128 * CW, [[CW, 128], [1, CW]]),
                        buf[:, p * CW:(p + 1) * CW])

        if not convert and hasattr(self, "wb_total"):
            for sm, vv in self.wb_total:
                self.w("y", sm, vv)
        issue_loads(0)
        if HCN > 1:
            issue_loads(1)
        HB = lambda cc, i: (2 * TH * cc + i) % 6
        NPC = 16
        w2n = self.c["HID"] * self.c["D"]
        for cc in range(HCN):
            p = cc % 2
            if convert and cc >= 2 and (cc - 2) % 2 == 0 and (cc - 2) // 2 < NPC:
                i = (cc - 2) // 2
                per = w2n // NPC
                self.dma("g", self.W2C,
                         bass.AP(self.w2bf, i * per, [[per // 128, 128], [1, per // 128]]),
                         bass.AP(self.w2t, i * per, [[per // 128, 128], [1, per // 128]]))
            if cc >= 1:
                self._emit_s2(cc - 1, mul_cnt, s2_mm)
            self.w("t", wld[cc][0], wld[cc][1])
            self.w("t", wld[cc][2], wld[cc][3])
            if cc == 0:
                self.w("t", self.S, s13_done)
            banks = [HB(cc, i) for i in range(2 * TH)]
            for b in banks:
                self._wait_fr("t", self.bank_free[b])
            for half, (wsb, sreg) in enumerate(((self.WS1, self.S1B), (self.WS3, self.S3B))):
                for w in range(TH):
                    bk = banks[half * TH + w]
                    for dc in range(DC):
                        self.o("t", "matmul", self.PS[bk][:, 0:W],
                               wsb[:, p * DC * 128 + dc * 128: p * DC * 128 + (dc + 1) * 128],
                               self.arena[:, dc * T + w * W: dc * T + (w + 1) * W],
                               start=(dc == 0), stop=False)
                    self.o("t", "matmul", self.PS[bk][:, 0:W],
                           self.B13S[0:16, p * 256 + half * 128: p * 256 + half * 128 + 128],
                           sreg[0:16, w * W:(w + 1) * W],
                           start=False, stop=True, inc=self.T)
                    h_mm[(cc, half, w)] = self.T.n
            h_mm[cc] = self.T.n
            mk = (self.T, h_mm[cc])
            if convert:
                mk = ("both", mk, (self.WB[p], wbst[cc]))
            self.fr["WS1"][p] = mk
            self.fr["WS3"][p] = mk
            self.fr["B13S"][p] = (self.T, h_mm[cc])
            if cc + 2 < HCN:
                issue_loads(cc + 2)
            self._wait_fr("s", self.fr["SIL"][p])
            self._wait_fr("s", self.fr["H3B"][p])
            for w in range(TH):
                bk = banks[w]
                self.w("s", self.T, h_mm[(cc, 0, w)])
                e = self.o("s", "activation", self.SIL[:, p * T + w * W: p * T + (w + 1) * W],
                           self.PS[bk][:, 0:W], AF.Silu, inc=self.S)
                self.bank_free[bk] = (self.S, e)
            for w in range(TH):
                bk = banks[TH + w]
                self.w("s", self.T, h_mm[(cc, 1, w)])
                e = self.o("s", "activation", self.H3B[:, p * T + w * W: p * T + (w + 1) * W],
                           self.PS[bk][:, 0:W], AF.Copy, scale=0.5, inc=self.S)
                self.bank_free[bk] = (self.S, e)
            ev_done = self.S.n
            self._wait_fr("v", self.fr["GB"][p])
            self.w("v", self.S, ev_done)
            mul_cnt[cc] = self.o("v", "tensor_mul", self.GB[:, p * T:(p + 1) * T],
                                 self.SIL[:, p * T:(p + 1) * T],
                                 self.H3B[:, p * T:(p + 1) * T], inc=self.V)
            self.fr["SIL"][p] = (self.V, mul_cnt[cc])
            self.fr["H3B"][p] = (self.V, mul_cnt[cc])
            self.w("y", self.V, mul_cnt[cc])
            if hasattr(self, "gbuf_read_done"):
                for sm, vv in self.gbuf_read_done:
                    self.w("y", sm, vv)
                del self.gbuf_read_done
            gst[cc] = self.dma("y", self.GST[p],
                               bass.AP(self.gbuf, cc * 128 * T, [[T, 128], [1, T]]),
                               self.GB[:, p * T:(p + 1) * T])
            self.fr["GB"][p] = (self.GST[p], gst[cc])
        self._emit_s2(HCN - 1, mul_cnt, s2_mm)
        for w in range(TH):
            bk = 6 + w
            self.w("s", self.T, s2_mm[HCN - 1])
            e = self.o("s", "activation", self.S2B[:, w * W:(w + 1) * W],
                       self.PS[bk][0:16, 0:W], AF.Copy, scale=2.0, inc=self.S)
            self.bank_free[bk] = (self.S, e)
        self.s2_ready = self.S.n
        self.gst = gst
        if convert:
            done = max(0, min(NPC, (HCN - 2 + 1) // 2))
            per = w2n // NPC
            for i in range(done, NPC):
                self.dma("g", self.W2C,
                         bass.AP(self.w2bf, i * per, [[per // 128, 128], [1, per // 128]]),
                         bass.AP(self.w2t, i * per, [[per // 128, 128], [1, per // 128]]))
            self.w2c_total = self.W2C.n
            self.wb_total = [(self.WB[0], self.WB[0].n), (self.WB[1], self.WB[1].n)]

    def _emit_s2(self, cc, mul_cnt, s2_mm):
        c = self.c
        TH, W, T, HCN = c["TH"], c["W"], c["T"], c["HCN"]
        p = cc % 2
        self.w("t", self.V, mul_cnt[cc])
        for w in range(TH):
            self.o("t", "matmul", self.PS[6 + w][0:16, 0:W],
                   self.A2S[:, cc * 16:(cc + 1) * 16],
                   self.GB[:, p * T + w * W: p * T + (w + 1) * W],
                   start=(cc == 0), stop=(cc == HCN - 1),
                   inc=(self.T if w == TH - 1 else None))
        s2_mm[cc] = self.T.n
        self.fr["GB"][p] = ("both", (self.T, s2_mm[cc]), self.fr["GB"][p])

    def _wait_fr(self, e, marker):
        if isinstance(marker[0], str) and marker[0] == "both":
            self._wait_fr(e, marker[1])
            self._wait_fr(e, marker[2])
        else:
            self.w(e, marker[0], marker[1])

    def phase_c(self, xsrc, xdst, weight, nchunks, lhs_of_chunk, stream_g, lora,
                src_ready, final=False, wbf=None, gsrc=None, gready=None):
        c = self.c
        NS, T, D, DG = c["NS"], c["T"], c["D"], c["DG"]
        ntot = weight.shape[1]
        gld = {}
        w2ld = {}
        mmd = {}
        add_cnt = {}
        xost = {}
        xcb = {}
        store_ready = [None] * NS

        def emit_xc_loads(dg):
            for tau in range(NS):
                self._wait_fr("y", self.fr["XC8"][tau])
                if src_ready is not None and src_ready[tau] is not None:
                    self.w("y", src_ready[tau][0], src_ready[tau][1])
                xcb[dg] = self.dma(
                    "y", self.XCs[dg % 2], self.XT[:, tau * 512:(tau + 1) * 512],
                    bass.AP(xsrc, tau * 128 * D + dg * 512, [[D, 128], [1, 512]]))

        for dg in range(DG):
            for cc in range(nchunks):
                p = cc % 3
                p4 = cc % 3
                if cc == min(2, nchunks - 1):
                    emit_xc_loads(dg)
                if stream_g:
                    self._wait_fr("y", self.fr["GS"][p])
                    if dg == 0:
                        for sm, vv in gready(cc):
                            self.w("y", sm, vv)
                    gld[cc] = self.dma("y", self.GSs[p], self.GS[:, p * T:(p + 1) * T],
                                       bass.AP(gsrc, cc * 128 * T, [[T, 128], [1, T]]))
                if wbf is not None:
                    self._wait_fr("y", self.fr["W2S"][p4])
                    if dg == 0 and cc == 0:
                        self.w("y", self.W2C, self.w2c_total)
                    w2ld[cc] = self.dma("y", self.W2m[p4],
                                        self.W2S[:, p4 * 512:(p4 + 1) * 512],
                                        bass.AP(wbf, cc * 128 * ntot + dg * 512,
                                                [[ntot, 128], [1, 512]]))
                else:
                    self._wait_fr("g", self.fr["W2S"][p4])
                    w2ld[cc] = self.dma("g", self.W2g[p4],
                                        self.W2S[:, p4 * 512:(p4 + 1) * 512],
                                        bass.AP(weight, cc * 128 * ntot + dg * 512,
                                                [[ntot, 128], [1, 512]]))
                if stream_g:
                    self.w("t", self.GSs[p], gld[cc])
                self.w("t", self.W2m[p4] if wbf is not None else self.W2g[p4], w2ld[cc])
                if cc == 0:
                    for tau in range(NS):
                        self._wait_fr("t", self.bank_free[tau])
                    if not stream_g:
                        self.w("t", self.V, self.afm_ready)
                for tau in range(NS):
                    self.o("t", "matmul", self.PS[tau][:, 0:512],
                           lhs_of_chunk(cc, tau), self.W2S[:, p4 * 512:(p4 + 1) * 512],
                           start=(cc == 0),
                           stop=(not lora) and (cc == nchunks - 1),
                           inc=(self.T if tau == NS - 1 else None))
                mmd[cc] = self.T.n
                self.fr["W2S"][p4] = (self.T, mmd[cc])
                if stream_g:
                    self.fr["GS"][p] = (self.T, mmd[cc])
            taud = {}
            if lora:
                self.w("t", self.S, self.s2_ready)
                for tau in range(NS):
                    taud[tau] = self.o("t", "matmul", self.PS[tau][:, 0:512],
                                       self.S2B[0:16, tau * 128:(tau + 1) * 128],
                                       self.B2S[0:16, dg * 512:(dg + 1) * 512],
                                       start=False, stop=True, inc=self.T)
            else:
                for tau in range(NS):
                    taud[tau] = mmd[nchunks - 1]
            for tau in range(NS):
                p = tau % 2
                self.w("v", self.T, taud[tau])
                self.w("v", self.XCs[dg % 2], xcb[dg])
                self._wait_fr("v", self.fr["XO"][p])
                add_cnt[(dg, tau)] = self.o(
                    "v", "tensor_add", self.XO[:, p * 512:(p + 1) * 512],
                    self.PS[tau][:, 0:512], self.XT[:, tau * 512:(tau + 1) * 512],
                    inc=self.V)
                self.bank_free[tau] = (self.V, add_cnt[(dg, tau)])
                self.fr["XC8"][tau] = (self.V, add_cnt[(dg, tau)])
                self.w("y", self.V, add_cnt[(dg, tau)])
                xost[(dg, tau)] = self.dma(
                    "y", self.XOs[p],
                    bass.AP(xdst, tau * 128 * D + dg * 512, [[D, 128], [1, 512]]),
                    self.XO[:, p * 512:(p + 1) * 512])
                self.fr["XO"][p] = (self.XOs[p], xost[(dg, tau)])
        for tau in range(NS):
            store_ready[tau] = (self.XOs[tau % 2], xost[(DG - 1, tau)])
        self.fr["XT"] = [(self.V, add_cnt[(DG - 1, NS - 1)])] * c["NHALF"]
        if stream_g and gsrc is self.gbuf:
            self.gbuf_read_done = [(s, s.n) for s in self.GSs]
        if final:
            self.w("y", self.XOs[0], self.XOs[0].n)
            self.w("y", self.XOs[1], self.XOs[1].n)
        return store_ready

    def phase_proj(self, group, interleave_core=False):
        c = self.c
        DC, NH, NS, T, W, TH, D = c["DC"], c["NH"], c["NS"], c["T"], c["W"], c["TH"], c["D"]
        HID = c["HID"]
        if interleave_core:
            blocks = []
            for h in range(NH):
                blocks += [("k", h), ("v", h)]
        else:
            blocks = [("k", h) for h in range(NH)] + [("v", h) for h in range(NH)]
            if group == 0:
                blocks += [("q", h) for h in range(NH)]
        wld = {}
        mm_done = {}

        CW = DC * 128

        def issue_loads(bi):
            kind, h = blocks[bi]
            p = bi % 2
            self._wait_fr("g", self.fr["WS1"][p])
            mat, blk = {
                "k": (self.wkv, h),
                "v": (self.wkv, NH + h),
                "q": (self.wq, h),
            }[kind]
            wld[bi] = self.dma(
                "g", self.WSm[p], self.WS1[:, p * CW:(p + 1) * CW],
                bass.AP(mat, blk * 128 * CW, [[CW, 128], [1, CW]]))

        issue_loads(0)
        issue_loads(1)
        nb = 0
        for bi, (kind, h) in enumerate(blocks):
            p = bi % 2
            self.w("t", self.WSm[p], wld[bi])
            if bi == 0:
                self.w("t", self.V, self.xln_ready)
            banks = [(nb + w) % 4 for w in range(TH)]
            nb += TH
            for w in range(TH):
                bk = banks[w]
                self._wait_fr("t", self.bank_free[bk])
                for dc in range(DC):
                    self.o("t", "matmul", self.PS[bk][:, 0:W],
                           self.WS1[:, p * DC * 128 + dc * 128: p * DC * 128 + (dc + 1) * 128],
                           self.arena[:, dc * T + w * W: dc * T + (w + 1) * W],
                           start=(dc == 0), stop=(dc == DC - 1),
                           inc=(self.T if dc == DC - 1 else None))
            mm_done[bi] = self.T.n
            self.fr["WS1"][p] = (self.T, mm_done[bi])
            if bi + 2 < len(blocks):
                issue_loads(bi + 2)
            self._wait_fr("s", self.fr["SIL"][p])
            evs = []
            for w in range(TH):
                bk = banks[w]
                self.w("s", self.T, mm_done[bi] - (TH - 1 - w))
                kwq = {"scale": SCALE_Q} if kind == "q" else {}
                e = self.o("s", "activation", self.SIL[:, p * T + w * W: p * T + (w + 1) * W],
                           self.PS[bk][:, 0:W], AF.Copy, inc=self.S, **kwq)
                self.bank_free[bk] = (self.S, e)
                evs.append(e)
            ev = evs[-1]
            if kind == "k":
                self.w("y", self.S, ev)
                stc = self.dma("y", self.SILs[p],
                               bass.AP(self.kloc, h * 128 * 2 * T + group * T,
                                       [[2 * T, 128], [1, T]]),
                               self.SIL[:, p * T:(p + 1) * T])
                self.kvready.setdefault((group, h), []).append((self.SILs[p], stc))
                self.fr["SIL"][p] = (self.SILs[p], stc)
            elif kind == "q":
                self.w("y", self.S, ev)
                stc = self.dma("y", self.SILs[p],
                               bass.AP(self.qloc, h * 128 * T, [[T, 128], [1, T]]),
                               self.SIL[:, p * T:(p + 1) * T])
                self.qready[h] = (self.SILs[p], stc)
                self.fr["SIL"][p] = (self.SILs[p], stc)
            else:
                bkT = 4 + (bi % 2)
                self._wait_fr("t", self.bank_free[bkT])
                self.w("t", self.S, ev)
                for tau in range(NS):
                    self.o("t", "transpose", self.PSb[bkT][:, (tau % 8) * 128:(tau % 8) * 128 + 128],
                           self.SIL[:, p * T + tau * 128: p * T + (tau + 1) * 128],
                           self.IDN[:, :],
                           inc=(self.T if (tau % 8 == 7 or tau == NS - 1) else None))
                    if tau % 8 == 7 or tau == NS - 1:
                        base = (tau // 8) * 8
                        cnt = tau - base + 1
                        self.w("v", self.T, self.T.n)
                        self._wait_fr("v", self.fr["H3B"][p])
                        e2 = self.o("v", "tensor_copy",
                                    self.H3B[:, p * T + base * 128: p * T + (base + cnt) * 128],
                                    self.PSb[bkT][:, 0:cnt * 128], inc=self.V)
                        self.bank_free[bkT] = (self.V, e2)
                tr_done = self.T.n
                self.fr["SIL"][p] = (self.T, tr_done)
                self.w("y", self.V, self.V.n)
                stc = self.dma("y", self.VSTs[p],
                               bass.AP(self.vloc, h * 2 * T * 128 + group * T * 128,
                                       [[128, 128], [128 * 128, NS], [1, 128]]),
                               self.H3B[:, p * T:(p + 1) * T])
                self.kvready.setdefault((group, h), []).append((self.VSTs[p], stc))
                self.fr["H3B"][p] = (self.VSTs[p], stc)
            if interleave_core and kind == "v":
                self._core_loads(h)
                if h >= 1:
                    self._core_compute(h - 1)
        if interleave_core:
            self._core_compute(NH - 1)
        self.proj_done = self.T.n

    def core_init(self):
        self._sc_bank = 0
        self._pt_bank = 0
        self._kvh = {}
        self._qst = {}
        self.afm_ready_h = {}

    def _core_loads(self, h):
        c = self.c
        NS, T = c["NS"], c["T"]
        p = h % 2
        self._wait_fr("y", self.fr["QH"][p])
        qs, qv = self.qready[h]
        self.w("y", qs, qv)
        for sm, vv in self.kvready[(0, h)] + self.kvready[(1, h)]:
            self.w("y", sm, vv)
        self.dma("y", self.KV[p], self.QH[:, p * T:(p + 1) * T],
                 bass.AP(self.qloc, h * 128 * T, [[T, 128], [1, T]]))
        self._wait_fr("y", self.fr["KH"][p])
        self.dma("y", self.KV[p], self.KH[:, p * 2 * T:(p + 1) * 2 * T],
                 bass.AP(self.kloc, h * 128 * 2 * T, [[2 * T, 128], [1, 2 * T]]))
        self._wait_fr("y", self.fr["VH"][p])
        self._kvh[h] = self.dma(
            "y", self.KV[p],
            self.VH[:, p * 2 * T:(p + 1) * 2 * T],
            bass.AP(self.vloc, h * 2 * T * 128,
                    [[128, 128], [128 * 128, 2 * NS], [1, 128]]))

    def _core_qt_a(self, h, j):
        c = self.c
        NS, T = c["NS"], c["T"]
        p = h % 2
        pp = (h * NS + j) % 2
        L = 128 * (j + 1)
        WJ = 2 * L
        ncch = (WJ + 511) // 512
        SB = pp * 24
        chunk_banks = []
        chunk_T = []
        for ch in range(ncch):
            c0 = ch * 512
            cw = min(512, WJ - c0)
            bk = self._sc_bank % 4
            self._sc_bank += 1
            self._wait_fr("t", self.bank_free[bk])
            frs = []
            if c0 < L:
                e = min(L, c0 + cw)
                frs.append((c0, e, c0))
            if c0 + cw > L:
                b = max(L, c0)
                frs.append((b, c0 + cw, T + (b - L)))
            for fi, (a, b, src) in enumerate(frs):
                self.o("t", "matmul", self.PS[bk][:, a - c0: b - c0],
                       self.QH[:, p * T + j * 128: p * T + (j + 1) * 128],
                       self.KH[:, p * 2 * T + src: p * 2 * T + src + (b - a)],
                       start=(fi == 0), stop=(fi == len(frs) - 1),
                       inc=(self.T if fi == len(frs) - 1 else None))
            chunk_banks.append((bk, c0, cw))
            chunk_T.append(self.T.n)
        mk_v = {}
        for mi, pos in enumerate((128 * j, L + 128 * j)):
            ch = pos // 512
            bk, c0, cw = chunk_banks[ch]
            self.w("v", self.T, chunk_T[ch])
            mk_v[ch] = self.o("v", "tensor_tensor",
                              self.PS[bk][:, pos - c0: pos - c0 + 128],
                              self.PS[bk][:, pos - c0: pos - c0 + 128],
                              self.MSKS[:, j * 256 + mi * 128: j * 256 + (mi + 1) * 128],
                              ALU.add, inc=self.V)
        exp_S = []
        for ch, (bk, c0, cw) in enumerate(chunk_banks):
            self.w("v", self.T, chunk_T[ch])
            if ch in mk_v:
                self.w("v", self.V, mk_v[ch])
            mx = self.o("v", "reduce_max", self.SFX[:, SB + ch: SB + ch + 1],
                        self.PS[bk][:, 0:cw], AX.X, negate=True, inc=self.V)
            self.w("s", self.V, mx)
            if ch == 0:
                self._wait_fr("s", self.fr["PT"][pp])
            e = self.o("s", "activation", self.PT[:, pp * 2 * T + c0: pp * 2 * T + c0 + cw],
                       self.PS[bk][:, 0:cw], AF.Exp,
                       bias=self.SFX[:, SB + ch: SB + ch + 1],
                       accum_out=self.SFX[:, SB + 8 + ch: SB + 9 + ch], inc=self.S)
            exp_S.append(e)
            self.bank_free[bk] = (self.S, e)
        self._qst[(h, j)] = (chunk_banks, exp_S)

    def _core_qt_b(self, h, j):
        c = self.c
        NH, NS, T = c["NH"], c["NS"], c["T"]
        p = h % 2
        pp = (h * NS + j) % 2
        L = 128 * (j + 1)
        WJ = 2 * L
        ncch = (WJ + 511) // 512
        SB = pp * 24
        chunk_banks, exp_S = self._qst.pop((h, j))
        if ncch == 1:
            self.w("v", self.S, exp_S[-1])
            self.o("v", "reciprocal", self.SFX[:, SB + 4: SB + 5],
                   self.SFX[:, SB + 8: SB + 9], inc=self.V)
            sc = self.V.n
        else:
            self.w("v", self.S, exp_S[-1])
            self.o("v", "tensor_reduce", self.SFX[:, SB + 16: SB + 17],
                   self.SFX[:, SB: SB + ncch], AX.X, ALU.min, inc=self.V)
            self.w("s", self.V, self.V.n)
            dl = self.o("s", "activation", self.SFX[:, SB + 4: SB + 4 + ncch],
                        self.SFX[:, SB: SB + ncch], AF.Exp,
                        bias=self.SFX[:, SB + 16: SB + 17], scale=-1.0, inc=self.S)
            self.w("v", self.S, dl)
            self.o("v", "tensor_mul", self.SFX[:, SB + 12: SB + 12 + ncch],
                   self.SFX[:, SB + 4: SB + 4 + ncch],
                   self.SFX[:, SB + 8: SB + 8 + ncch], inc=self.V)
            self.w("v", self.V, self.V.n)
            self.o("v", "reduce_sum", self.SFX[:, SB + 17: SB + 18],
                   self.SFX[:, SB + 12: SB + 12 + ncch], AX.X, inc=self.V)
            self.w("v", self.V, self.V.n)
            self.o("v", "reciprocal", self.SFX[:, SB + 18: SB + 19],
                   self.SFX[:, SB + 17: SB + 18], inc=self.V)
            self.w("v", self.V, self.V.n)
            sc = self.o("v", "tensor_scalar", self.SFX[:, SB + 4: SB + 4 + ncch],
                        self.SFX[:, SB + 4: SB + 4 + ncch],
                        self.SFX[:, SB + 18: SB + 19], None, ALU.mult, inc=self.V)
        pn = []
        for ch, (bk, c0, cw) in enumerate(chunk_banks):
            self.w("v", self.V, sc)
            e = self.o("v", "tensor_scalar",
                       self.PT[:, pp * 2 * T + c0: pp * 2 * T + c0 + cw],
                       self.PT[:, pp * 2 * T + c0: pp * 2 * T + c0 + cw],
                       self.SFX[:, SB + 4 + ch: SB + 5 + ch], None, ALU.mult,
                       inc=self.V)
            pn.append(e)
        nkt = 2 * (j + 1)
        tev = []
        i = 0
        while i < nkt:
            nb2 = min(8, nkt - i)
            bkT = 4 + (self._pt_bank % 2)
            self._pt_bank += 1
            self._wait_fr("t", self.bank_free[bkT])
            self.w("t", self.V, pn[-1])
            for q in range(nb2):
                kt = i + q
                self.o("t", "transpose", self.PSb[bkT][:, q * 128:(q + 1) * 128],
                       self.PT[:, pp * 2 * T + kt * 128: pp * 2 * T + (kt + 1) * 128],
                       self.IDN[:, :], inc=(self.T if q == nb2 - 1 else None))
            tcnt = self.T.n
            if i == 0:
                self._wait_fr("v", self.fr["PF"][pp])
            self.w("v", self.T, tcnt)
            e = self.o("v", "tensor_copy",
                       self.PF[:, pp * 2 * T + i * 128: pp * 2 * T + (i + nb2) * 128],
                       self.PSb[bkT][:, 0:nb2 * 128], inc=self.V)
            tev.append(e)
            self.bank_free[bkT] = (self.V, e)
            i += nb2
        self.fr["PT"][pp] = (self.T, self.T.n)
        g4 = min(4, NS)
        bkA = 6 + ((j // g4) % 2)
        col = (j % g4) * 128
        if j % g4 == 0:
            self._wait_fr("t", self.bank_free[bkA])
        self.w("t", self.V, tev[-1])
        for kt in range(nkt):
            r = 0 if kt <= j else 1
            sslot = kt if kt <= j else kt - (j + 1)
            vcol = p * 2 * T + (r * NS + sslot) * 128
            self.o("t", "matmul", self.PS[bkA][:, col: col + 128],
                   self.VH[:, vcol: vcol + 128],
                   self.PF[:, pp * 2 * T + kt * 128: pp * 2 * T + (kt + 1) * 128],
                   start=(kt == 0), stop=(kt == nkt - 1),
                   inc=(self.T if kt == nkt - 1 else None))
        avd = self.T.n
        self.fr["PF"][pp] = (self.T, avd)
        if j % g4 == g4 - 1 or j == NS - 1:
            jb = (j // g4) * g4
            ncol = (j - jb + 1) * 128
            par = (h * ((NS + g4 - 1) // g4) + j // g4) % 2
            self.w("v", self.T, avd)
            self._wait_fr("v", self.fr["AFS"][par])
            e = self.o("v", "tensor_copy",
                       self.AFS[:, par * 512: par * 512 + ncol],
                       self.PS[bkA][:, 0:ncol], inc=self.V)
            self.bank_free[bkA] = (self.V, e)
            self.w("y", self.V, e)
            stc = self.dma("y", self.AFSt[par],
                           bass.AP(self.afm, h * 128 * T + jb * 128,
                                   [[T, 128], [1, ncol]]),
                           self.AFS[:, par * 512: par * 512 + ncol])
            self.fr["AFS"][par] = (self.AFSt[par], stc)
            self.afm_ready_h.setdefault(h, []).append((self.AFSt[par], stc))

    def _core_compute(self, h):
        c = self.c
        NS, T = c["NS"], c["T"]
        p = h % 2
        self.w("t", self.KV[p], self._kvh[h])
        self._core_qt_a(h, 0)
        if NS > 1:
            self._core_qt_a(h, 1)
        for j in range(NS):
            self._core_qt_b(h, j)
            if j + 2 < NS:
                self._core_qt_a(h, j + 2)
        self.fr["QH"][p] = (self.T, self.T.n)
        self.fr["KH"][p] = (self.T, self.T.n)
        self.fr["VH"][p] = (self.T, self.T.n)

    def finalize(self):
        nc = self.nc
        for s in self.sems:
            s.h = nc.alloc_semaphore(s.name)
        with nc.Block() as blk:
            blk.tensor(lambda E, f=self.prog["t"]: [fn(E) for fn in f])
            blk.vector(lambda E, f=self.prog["v"]: [fn(E) for fn in f])
            blk.scalar(lambda E, f=self.prog["s"]: [fn(E) for fn in f])
            blk.sync(lambda E, f=self.prog["y"]: [fn(E) for fn in f])
            blk.gpsimd(lambda E, f=self.prog["g"]: [fn(E) for fn in f])
        return nc


def build_program(cfg):
    c = _derived(cfg)
    c["NCORES"] = cfg["NCORES"]
    nc = bass.Bass("TRN2", target_bir_lowering=False)
    b = _B(nc, c)
    b.kvready = {}
    b.qready = {}
    b.prelude()
    gs_lhs = lambda cc, tau: b.GS[:, (cc % 3) * c["T"] + tau * 128: (cc % 3) * c["T"] + (tau + 1) * 128]
    gb_ready = lambda cc: [(b.GST[cc % 2], b.gst[cc])]
    b.phase_a(b.x0, None)
    b.phase_b(convert=True)
    r1 = b.phase_c(b.x0, b.xr1, b.w2t, c["HCN"], gs_lhs, True, True, None,
                   wbf=b.w2bf, gsrc=b.gbuf, gready=gb_ready)
    b.phase_a(b.x0p, None)
    b.phase_b()
    r1p = b.phase_c(b.x0p, b.xr1p, b.w2t, c["HCN"], gs_lhs, True, True, None,
                    wbf=b.w2bf, gsrc=b.gbuf, gready=gb_ready)
    b.phase_a(b.xr1, r1)
    b.phase_proj(0)
    b.phase_a(b.xr1p, r1p)
    b.core_init()
    b.phase_proj(1, interleave_core=True)
    r2 = b.phase_c(b.xr1, b.xr2, b.wo, c["NH"], gs_lhs, True, False, r1,
                   gsrc=b.afm, gready=lambda cc: b.afm_ready_h[cc])
    b.phase_a(b.xr2, r2)
    b.phase_b()
    b.phase_c(b.xr2, b.out, b.w2t, c["HCN"], gs_lhs, True, True, r2, final=True,
              wbf=b.w2bf, gsrc=b.gbuf, gready=gb_ready)
    return b.finalize()



def own_tile(r, j):
    return 2 * j + ((j & 1) ^ r)


def build_masks(cfg, r):
    c = _derived(cfg)
    NS = c["NS"]
    msk = np.zeros((NS * 128, 256), dtype=ml_dtypes.bfloat16)
    tri = np.where(np.arange(128)[None, :] <= np.arange(128)[:, None], 0.0, NEG
                   ).astype(ml_dtypes.bfloat16)
    for j in range(NS):
        g_self = own_tile(r, j)
        g_sib = own_tile(1 - r, j)
        msk[j * 128:(j + 1) * 128, 0:128] = tri
        if g_sib > g_self:
            msk[j * 128:(j + 1) * 128, 128:256] = NEG
    return msk


def host_layout(cfg, canon, ncores):
    c = _derived(cfg)
    NS, D, T = c["NS"], c["D"], c["T"]
    x = np.ascontiguousarray(canon["x"], dtype=np.float32)
    B, S, _ = x.shape
    assert S == 2 * T and B == ncores // 2
    ident = np.eye(128, dtype=np.float32)
    DC, HCN, NH = c["DC"], c["HCN"], c["NH"]

    def chunk_major(w, nblk):
        return np.ascontiguousarray(
            np.asarray(w).reshape(DC, 128, nblk, 128).transpose(2, 1, 0, 3)
            .reshape(nblk * 128, DC * 128))

    shared = {
        "w1t": chunk_major(canon["w1t"], HCN),
        "w3t": chunk_major(canon["w3t"], HCN),
        "w2t": canon["w2t"],
        "wq": chunk_major(canon["wq"], NH),
        "wkv": chunk_major(canon["wkv"], 2 * NH),
        "wo": canon["wo"],
        "a13": canon["a13"], "b13": canon["b13"], "a2": canon["a2t"],
        "b2": canon["b2t"], "idn": ident,
    }
    shared = {k: np.ascontiguousarray(v, dtype=np.float32) for k, v in shared.items()}
    in_maps = []
    for core in range(ncores):
        bidx, r = core // 2, core % 2
        tiles = [own_tile(r, j) for j in range(NS)]
        ptiles = [own_tile(1 - r, j) for j in range(NS)]
        xr = x[bidx].reshape(2 * NS, 128, D)
        m = dict(shared)
        m["x0"] = np.ascontiguousarray(xr[tiles].reshape(T, D))
        m["x0p"] = np.ascontiguousarray(xr[ptiles].reshape(T, D))
        m["msk"] = build_masks(cfg, r)
        in_maps.append(m)
    return in_maps


def assemble_out(cfg, outs, ncores):
    c = _derived(cfg)
    NS, D, T = c["NS"], c["D"], c["T"]
    B = ncores // 2
    full = np.zeros((B, 2 * NS, 128, D), dtype=np.float32)
    for core in range(ncores):
        bidx, r = core // 2, core % 2
        o = outs[core].reshape(NS, 128, D)
        for j in range(NS):
            full[bidx, own_tile(r, j)] = o[j]
    return full.reshape(B, 2 * NS * 128, D)


def canonicalize(inputs):
    H, DN, DR, DV = 32, 128, 64, 128
    DQ = DN + DR
    wq = inputs["L0_attn_wq"]
    D = wq.shape[0]
    wq_used = np.ascontiguousarray(
        np.asarray(wq).reshape(D, H, DQ)[:, :, :DN].reshape(D, H * DN))
    wkv = np.asarray(inputs["L0_attn_wkv"])
    wkv_used = np.ascontiguousarray(wkv[:, :H * (DN + DV)])
    w1 = np.asarray(inputs["L0_ffn_w1"])
    w3 = np.asarray(inputs["L0_ffn_w3"])
    w2 = np.asarray(inputs["L0_ffn_w2"])
    a1 = np.asarray(inputs["L0_ffn_lora_A1"])
    a3 = np.asarray(inputs["L0_ffn_lora_A3"])
    b1 = np.asarray(inputs["L0_ffn_lora_B1"])
    b3 = np.asarray(inputs["L0_ffn_lora_B3"])
    a2 = np.asarray(inputs["L0_ffn_lora_A2"])
    b2 = np.asarray(inputs["L0_ffn_lora_B2"])
    canon = dict(
        x=np.asarray(inputs["L0_attn_hidden_states"]),
        w1t=np.ascontiguousarray(w1.T),
        w3t=np.ascontiguousarray(w3.T),
        w2t=np.ascontiguousarray(w2.T),
        a13=np.ascontiguousarray(np.concatenate(
            [a1.T.reshape(D, 16), a3.T.reshape(D, 16)], axis=1)
            .reshape(D, 32)),
        b13=np.ascontiguousarray(np.concatenate([b1.T, b3.T], axis=1)),
        a2t=np.ascontiguousarray(a2.T),
        b2t=np.ascontiguousarray(b2.T),
        wq=wq_used,
        wkv=wkv_used,
        wo=np.ascontiguousarray(np.asarray(inputs["L0_attn_wo"])),
    )
    return canon


def kernel(**inputs):
    cfg = full_cfg()
    canon = canonicalize(inputs)
    in_maps = host_layout(cfg, canon, cfg["NCORES"])
    nc = build_program(cfg)
    res = run_bass_kernel_spmd(nc, in_maps, core_ids=list(range(cfg["NCORES"])))
    outs = [res.results[i]["out"] for i in range(cfg["NCORES"])]
    return assemble_out(cfg, outs, cfg["NCORES"]).astype(np.float32)
